# version 03
# baseline (speedup 1.0000x reference)
import math
from contextlib import ExitStack
import numpy as np
import concourse.bass as bass
import concourse.mybir as mybir
from concourse.bass_utils import run_bass_kernel_spmd

F32 = mybir.dt.float32
BF16 = mybir.dt.bfloat16
I32 = mybir.dt.int32
ALU = mybir.AluOpType
AF = mybir.ActivationFunctionType
AX = mybir.AxisListType

NCORES = 8
B = 4
TFULL = 4096
D = 1024
TC = 2048
TB = 512
NCH = 8
LN_EPS = 1e-5
LRC = 0.001 * 2.0 / (4096.0 * 1024.0)
QS = 1.0 / 127.0
NEG = -1.0e30
RG = [list(range(NCORES))]
PH_ALL = 99
DUMP = False


class Sem:
    def __init__(self, nc, name):
        self.h = nc.alloc_semaphore(name)
        self.v = 0

    def inc(self, inst, n=1):
        inst.then_inc(self.h, n)
        self.v += n
        return self.v

    def w(self, eng, val=None):
        eng.wait_ge(self.h, self.v if val is None else val)


def build_nc(upto=PH_ALL):
    nc = bass.Bass("TRN2", target_bir_lowering=False)

    xT = nc.dram_tensor("xT", [D, TC], F32, kind="ExternalInput")
    tkT = nc.dram_tensor("tkT", [D, D], F32, kind="ExternalInput")
    tvT = nc.dram_tensor("tvT", [D, D], F32, kind="ExternalInput")
    tqT = nc.dram_tensor("tqT", [D, D], F32, kind="ExternalInput")
    w1T = nc.dram_tensor("w1T", [D, 256], F32, kind="ExternalInput")
    w2T = nc.dram_tensor("w2T", [256, D], F32, kind="ExternalInput")
    b1h = nc.dram_tensor("b1h", [128, 2], F32, kind="ExternalInput")
    b2h = nc.dram_tensor("b2h", [128, 8], F32, kind="ExternalInput")
    wgT = nc.dram_tensor("wgT", [D, 3072], I32, kind="ExternalInput")
    wuT = nc.dram_tensor("wuT", [D, 3072], I32, kind="ExternalInput")
    wdT = nc.dram_tensor("wdT", [3072, D], I32, kind="ExternalInput")
    wattn = nc.dram_tensor("wattn", [D, 3 * D], F32, kind="ExternalInput")
    battn = nc.dram_tensor("battn", [128, 24], F32, kind="ExternalInput")
    wproj = nc.dram_tensor("wproj", [D, D], F32, kind="ExternalInput")
    bproj = nc.dram_tensor("bproj", [128, 8], F32, kind="ExternalInput")
    btab = nc.dram_tensor("btab", [32, 16], F32, kind="ExternalInput")
    rwT = nc.dram_tensor("rwT", [D, 16], F32, kind="ExternalInput")
    rbias = nc.dram_tensor("rbias", [16, 1], F32, kind="ExternalInput")
    up_p = nc.dram_tensor("up_p", [16, D, 512], F32, kind="ExternalInput")
    gate_p = nc.dram_tensor("gate_p", [16, D, 512], F32, kind="ExternalInput")
    down_p = nc.dram_tensor("down_p", [16, 512, D], F32, kind="ExternalInput")
    shgT = nc.dram_tensor("shgT", [D, 1024], F32, kind="ExternalInput")
    shuT = nc.dram_tensor("shuT", [D, 1024], F32, kind="ExternalInput")
    shdT = nc.dram_tensor("shdT", [1024, D], F32, kind="ExternalInput")
    ident_in = nc.dram_tensor("ident_in", [128, 128], F32, kind="ExternalInput")
    mmask = nc.dram_tensor("mmask", [128, 512], F32, kind="ExternalInput")
    onehotT = nc.dram_tensor("onehotT", [32, 128 * 512], F32, kind="ExternalInput")

    yout = nc.dram_tensor("yout", [D, TC], F32, kind="ExternalOutput")
    dbg1 = nc.dram_tensor("dbg1", [128, 4096], F32, kind="ExternalOutput")
    dbg2 = nc.dram_tensor("dbg2", [128, 4096], F32, kind="ExternalOutput")
    dbg3 = nc.dram_tensor("dbg3", [128, 2048], F32, kind="ExternalOutput")

    x_hbm = nc.dram_tensor("x_hbm", [D, TC], F32, kind="Internal")
    a_hbm = nc.dram_tensor("a_hbm", [16 * 128, 512], F32, kind="Internal")
    rows_hbm = nc.dram_tensor("rows_hbm", [2, TB], F32, kind="Internal")
    sc_hbm = nc.dram_tensor("sc_hbm", [2, 2], F32, kind="Internal")
    cmb_hbm = nc.dram_tensor("cmb_hbm", [16, TC], F32, kind="Internal")
    NAR = 1024 * 256 + 256 * 1024 + 1280
    ar_in = nc.dram_tensor("ar_in", [NAR], F32, kind="Internal")
    ar_out = nc.dram_tensor("ar_out", [NAR], F32, kind="Internal")
    ag1_in = nc.dram_tensor("ag1_in", [3 * D, TC // 2], I32, kind="Internal")
    ag1_out = nc.dram_tensor("ag1_out", [NCORES * 3 * D, TC // 2], I32, kind="Internal")
    ag2_in = nc.dram_tensor("ag2_in", [128, B * TFULL // 2], I32, kind="Internal")
    ag2_out = nc.dram_tensor("ag2_out", [NCORES * 128, B * TFULL // 2], I32, kind="Internal")
    ag3_in = nc.dram_tensor("ag3_in", [2, B * TFULL], F32, kind="Internal")
    ag3_out = nc.dram_tensor("ag3_out", [16, B * TFULL], F32, kind="Internal")
    myqkv = nc.dram_tensor("myqkv", [384, 8192], I32, kind="Internal")
    myy = nc.dram_tensor("myy", [1024, 1024], I32, kind="Internal")
    myl = nc.dram_tensor("myl", [16, TC], F32, kind="Internal")

    xn = nc.alloc_sbuf_tensor("xn", [128, NCH * TC], BF16)
    sc32 = nc.alloc_sbuf_tensor("sc32", [128, 512], F32)
    sc32b = nc.alloc_sbuf_tensor("sc32b", [128, 512], F32)
    mb = nc.alloc_sbuf_tensor("mb", [128, 512], F32)
    ib = nc.alloc_sbuf_tensor("ib", [128, 512], F32)
    r_sum = nc.alloc_sbuf_tensor("r_sum", [1, TB], F32)
    r_sq = nc.alloc_sbuf_tensor("r_sq", [1, TB], F32)
    r_mean = nc.alloc_sbuf_tensor("r_mean", [1, TB], F32)
    r_inv = nc.alloc_sbuf_tensor("r_inv", [1, TB], F32)
    xs_a = nc.alloc_sbuf_tensor("xs_a", [128, 512], F32)
    xs_d = nc.alloc_sbuf_tensor("xs_d", [128, 512], F32)
    xs_b = nc.alloc_sbuf_tensor("xs_b", [128, 512], BF16)
    xs_c = nc.alloc_sbuf_tensor("xs_c", [128, 512], BF16)
    ident = nc.alloc_sbuf_tensor("ident", [128, 128], BF16)
    identf = nc.alloc_sbuf_tensor("identf", [128, 128], F32)
    onescol = nc.alloc_sbuf_tensor("onescol", [128, 1], BF16)
    b1c = nc.alloc_sbuf_tensor("b1c", [128, 2], F32)
    b2c = nc.alloc_sbuf_tensor("b2c", [128, 8], F32)
    battnc = nc.alloc_sbuf_tensor("battnc", [128, 24], F32)
    bprojc = nc.alloc_sbuf_tensor("bprojc", [128, 8], F32)
    rbc = nc.alloc_sbuf_tensor("rbc", [16, 1], F32)
    epsc = nc.alloc_sbuf_tensor("epsc", [1, 1], F32)

    ps = [nc.alloc_psum_tensor(f"ps{i}", [128, 512], F32) for i in range(6)]

    s_ld = Sem(nc, "s_ld")
    s_wt = Sem(nc, "s_wt")
    s_mm = Sem(nc, "s_mm")
    s_ac = Sem(nc, "s_ac")
    s_vv = Sem(nc, "s_vv")
    s_st = Sem(nc, "s_st")
    s_cc = Sem(nc, "s_cc")
    s_lda = Sem(nc, "s_lda")
    s_ldb = Sem(nc, "s_ldb")
    s_bc = Sem(nc, "s_bc")
    s_sg = Sem(nc, "s_sg")
    s_wta = Sem(nc, "s_wta")
    s_wtb = Sem(nc, "s_wtb")

    from contextlib import contextmanager

    @contextmanager
    def NB():
        with nc.Block() as blk:
            vv, av = s_vv.v, s_ac.v

            def w2(e, vv=vv, av=av):
                e.wait_ge(s_vv.h, vv)
                e.wait_ge(s_ac.h, av)
            blk.tensor(w2)
            blk.sync(w2)
            blk.scalar(lambda e, vv=vv: e.wait_ge(s_vv.h, vv))
            blk.vector(lambda e, av=av: e.wait_ge(s_ac.h, av))
            blk.gpsimd(w2)
            yield blk

    def ln_pass(src_dram):
        for tt4 in range(4):
            c0 = tt4 * TB
            with NB() as blk:
                stat_mm = [0] * NCH
                cast_done = [0] * NCH
                for kk in range(NCH):
                    xbuf = xs_a if kk % 2 == 0 else xs_d
                    xsm = s_lda if kk % 2 == 0 else s_ldb

                    @blk.sync
                    def _(e, kk=kk, xbuf=xbuf, xsm=xsm):
                        if kk >= 2:
                            e.wait_ge(s_vv.h, cast_done[kk - 2])
                        xsm.inc(e.dma_start(xbuf[:], src_dram[kk * 128:(kk + 1) * 128, c0:c0 + TB]), 16)
                    ldv = xsm.v

                    @blk.vector
                    def _(e, kk=kk, ldv=ldv, xbuf=xbuf, xsm=xsm):
                        xsm.w(e, ldv)
                        if kk >= 1:
                            e.wait_ge(s_mm.h, stat_mm[kk - 1])
                        e.tensor_copy(xs_b[:], xbuf[:])
                        s_vv.inc(e.tensor_tensor(xs_c[:], xs_b[:], xs_b[:], ALU.mult), 1)
                        cast_done[kk] = s_vv.v
                    vv = s_vv.v

                    @blk.tensor
                    def _(e, kk=kk, vv=vv):
                        e.wait_ge(s_vv.h, vv)
                        e.matmul(ps[4][0:1, :], onescol[:], xs_b[:], start=(kk == 0), stop=(kk == NCH - 1))
                        s_mm.inc(e.matmul(ps[5][0:1, :], onescol[:], xs_c[:], start=(kk == 0), stop=(kk == NCH - 1)), 1)
                        stat_mm[kk] = s_mm.v
                mmv = s_mm.v

                @blk.vector
                def _(e, mmv=mmv):
                    e.wait_ge(s_mm.h, mmv)
                    e.tensor_scalar(r_mean[:], ps[4][0:1, :], 1.0 / D, None, ALU.mult)
                    s_vv.inc(e.tensor_scalar(r_sq[:], ps[5][0:1, :], 1.0 / D, None, ALU.mult), 1)
                    s_vv.w(e)
                    e.tensor_tensor(r_sum[:], r_mean[:], r_mean[:], ALU.mult)
                    s_vv.inc(e.tensor_tensor(r_sq[:], r_sq[:], r_sum[:], ALU.subtract), 1)
                vv = s_vv.v

                @blk.scalar
                def _(e, vv=vv):
                    e.wait_ge(s_vv.h, vv)
                    s_ac.inc(e.activation(r_sq[:], r_sq[:], AF.Sqrt, bias=epsc[:], scale=1.0), 1)
                acv = s_ac.v

                @blk.vector
                def _(e, acv=acv):
                    e.wait_ge(s_ac.h, acv)
                    s_vv.inc(e.reciprocal(r_inv[:], r_sq[:]), 1)
                vv = s_vv.v

                @blk.sync
                def _(e, vv=vv):
                    e.wait_ge(s_vv.h, vv)
                    s_bc.inc(e.dma_start(rows_hbm[0:1, :], r_mean[:]), 16)
                    s_bc.inc(e.dma_start(rows_hbm[1:2, :], r_inv[:]), 16)
                    s_bc.w(e)
                    s_bc.inc(e.dma_start(mb[:], rows_hbm[0:1, :].broadcast_to((128, TB))), 16)
                    s_bc.inc(e.dma_start(ib[:], rows_hbm[1:2, :].broadcast_to((128, TB))), 16)
                bcv = s_bc.v

                apply_done = [0] * NCH
                for kk in range(NCH):
                    xbuf = xs_a if kk % 2 == 0 else xs_d
                    xsm = s_lda if kk % 2 == 0 else s_ldb

                    @blk.sync
                    def _(e, kk=kk, xbuf=xbuf, xsm=xsm):
                        if kk >= 2:
                            e.wait_ge(s_vv.h, apply_done[kk - 2])
                        xsm.inc(e.dma_start(xbuf[:], src_dram[kk * 128:(kk + 1) * 128, c0:c0 + TB]), 16)
                    ldv2 = xsm.v

                    @blk.vector
                    def _(e, kk=kk, ldv2=ldv2, xbuf=xbuf, xsm=xsm, bcv=bcv):
                        xsm.w(e, ldv2)
                        e.wait_ge(s_bc.h, bcv)
                        e.tensor_tensor(sc32[:], xbuf[:], mb[:], ALU.subtract)
                        s_vv.inc(e.tensor_tensor(xn[:, kk * TC + c0: kk * TC + c0 + TB], sc32[:], ib[:], ALU.mult), 1)
                        apply_done[kk] = s_vv.v

    def stream_w(blk, dst_ap, src_ap, guard=None, sem=None):
        sem = sem if sem is not None else s_wt
        mwid = src_ap.shape[1]
        src3 = src_ap.rearrange("(kk p) m -> p kk m", p=128)
        dst3 = dst_ap.rearrange("p (kk m) -> p kk m", m=mwid)

        @blk.gpsimd
        def _(e):
            if guard is not None:
                e.wait_ge(guard[0].h, guard[1])
            sem.inc(e.dma_start(dst3, src3), 16)
        return (sem, sem.v)

    def rearr(t):
        return t

    with NB() as blk:
        @blk.sync
        def _(e):
            s_ld.inc(e.dma_start(identf[:], ident_in[:]), 16)
            s_ld.inc(e.dma_start(b1c[:], b1h[:]), 16)
            s_ld.inc(e.dma_start(b2c[:], b2h[:]), 16)
            s_ld.inc(e.dma_start(battnc[:], battn[:]), 16)
            s_ld.inc(e.dma_start(bprojc[:], bproj[:]), 16)
            s_ld.inc(e.dma_start(rbc[:], rbias[:]), 16)

        @blk.gpsimd
        def _(e):
            s_wt.inc(e.dma_start(ident[:], ident_in[:]), 16)

        @blk.vector
        def _(e):
            e.memset(onescol[:], 1.0)
            e.memset(epsc[:], LN_EPS)

        @blk.sync
        def _(e):
            s_ld.w(e)

        @blk.gpsimd
        def _(e):
            s_wt.w(e)

    if upto >= 1:
        with nc.sbuf_tensor("btab_sb", [32, 16], BF16) as btab_sb, \
             nc.sbuf_tensor("oh_sb", [32, 4 * 512], BF16) as oh_sb, \
             nc.sbuf_tensor("ab_ev", [16, 512], F32) as ab_ev, \
             nc.Block() as blk:
            @blk.gpsimd
            def _(e):
                s_wt.inc(e.dma_start(btab_sb[:], btab[:]), 16)
            wtv0 = s_wt.v
            mm_done = [0] * 128
            ev_done = [0] * 128
            st_done = [0] * 128
            for p in range(128):
                buf = p % 4

                @blk.gpsimd
                def _(e, p=p, buf=buf):
                    if p >= 4:
                        e.wait_ge(s_mm.h, mm_done[p - 4])
                    s_wt.inc(e.dma_start(oh_sb[:, buf * 512:(buf + 1) * 512],
                                         onehotT[:, p * 512:(p + 1) * 512]), 16)
                wv = s_wt.v

                @blk.tensor
                def _(e, p=p, buf=buf, wv=wv):
                    e.wait_ge(s_wt.h, max(wv, wtv0))
                    if p >= 1:
                        e.wait_ge(s_ac.h, ev_done[p - 1])
                    s_mm.inc(e.matmul(ps[4][0:16, :], btab_sb[:], oh_sb[:, buf * 512:(buf + 1) * 512],
                                      start=True, stop=True), 1)
                    mm_done[p] = s_mm.v
                mv = s_mm.v

                @blk.scalar
                def _(e, p=p, mv=mv):
                    e.wait_ge(s_mm.h, mv)
                    if p >= 1:
                        e.wait_ge(s_ld.h, st_done[p - 1])
                    s_ac.inc(e.copy(ab_ev[:], ps[4][0:16, :]), 1)
                    ev_done[p] = s_ac.v
                av = s_ac.v

                @blk.sync
                def _(e, p=p, av=av):
                    e.wait_ge(s_ac.h, av)
                    s_ld.inc(e.dma_start(a_hbm[:, :].rearrange("(h p) j -> h (p j)", p=128)[:, p * 512:(p + 1) * 512],
                                         ab_ev[:]), 16)
                    st_done[p] = s_ld.v

            @blk.sync
            def _(e):
                s_ld.w(e)

        ln_pass(xT)

    if upto >= 2:
        with ExitStack() as _st:
            _e = _st.enter_context
            wstr = _e(nc.sbuf_tensor("wstr", [128, 1024], BF16))
            wstr2 = _e(nc.sbuf_tensor("wstr2", [128, 1024], BF16))
            w1t32 = _e(nc.sbuf_tensor("w1t32", [128, 2048], F32))
            w2t32 = _e(nc.sbuf_tensor("w2t32", [128, 2048], F32))
            w1tb = _e(nc.sbuf_tensor("w1tb", [128, 2048], BF16))
            w2tb = _e(nc.sbuf_tensor("w2tb", [128, 2048], BF16))
            w2b = _e(nc.sbuf_tensor("w2b", [128, 2048], BF16))
            zf = _e(nc.sbuf_tensor("zf", [128, 4096], F32))
            qf = _e(nc.sbuf_tensor("qf", [128, 4096], F32))
            zb = _e(nc.sbuf_tensor("zb", [128, 4096], BF16))
            qb = _e(nc.sbuf_tensor("qb", [128, 4096], BF16))
            grb = _e(nc.sbuf_tensor("grb", [128, 4096], BF16))
            af = _e(nc.sbuf_tensor("af", [128, 1024], F32))
            hb = _e(nc.sbuf_tensor("hb", [128, 1024], BF16))
            gp1 = _e(nc.sbuf_tensor("gp1", [128, 1024], F32))
            gp2 = _e(nc.sbuf_tensor("gp2", [128, 1024], F32))
            gab = _e(nc.sbuf_tensor("gab", [128, 1024], BF16))
            gsgn = _e(nc.sbuf_tensor("gsgn", [128, 1024], F32))
            ztm = _e(nc.sbuf_tensor("ztm", [128, 4096], BF16))
            grtm = _e(nc.sbuf_tensor("grtm", [128, 4096], BF16))
            htm = _e(nc.sbuf_tensor("htm", [128, 1024], BF16))
            gatm = _e(nc.sbuf_tensor("gatm", [128, 1024], BF16))
            dw1 = _e(nc.sbuf_tensor("dw1", [128, 2048], F32))
            dw2 = _e(nc.sbuf_tensor("dw2", [128, 2048], F32))
            dbs = _e(nc.sbuf_tensor("dbs", [128, 10], F32))
            psTb = _e(nc.psum_tensor("psTb", [128, 512], BF16))

            with NB() as blk:
                @blk.sync
                def _(e):
                    for kk in range(8):
                        s_ld.inc(e.dma_start(w1t32[:, kk * 256:(kk + 1) * 256],
                                             w1T[kk * 128:(kk + 1) * 128, :]), 16)
                    for kk in range(2):
                        s_ld.inc(e.dma_start(w2t32[:, kk * 1024:(kk + 1) * 1024],
                                             w2T[kk * 128:(kk + 1) * 128, :]), 16)
                ldv = s_ld.v

                @blk.vector
                def _(e, ldv=ldv):
                    s_ld.w(e, ldv)
                    for kk in range(8):
                        e.tensor_copy(w1tb[:, kk * 256:(kk + 1) * 256], w1t32[:, kk * 256:(kk + 1) * 256])
                    for kk in range(2):
                        s_vv.inc(e.tensor_copy(w2tb[:, kk * 1024:(kk + 1) * 1024],
                                               w2t32[:, kk * 1024:(kk + 1) * 1024]), 1)

            for step in range(B):
                c0 = step * TB

                with NB() as blk:
                    for kk in range(2):
                        for m in range(8):
                            @blk.tensor
                            def _(e, kk=kk, m=m):
                                e.wait_ge(s_vv.h, s_vv.v)
                                s_mm.inc(e.transpose(psTb[:, 0:128],
                                                     w2tb[:, kk * 1024 + m * 128: kk * 1024 + (m + 1) * 128],
                                                     ident[:]), 1)
                            mv = s_mm.v

                            @blk.vector
                            def _(e, kk=kk, m=m, mv=mv):
                                e.wait_ge(s_mm.h, mv)
                                s_vv.inc(e.tensor_copy(w2b[:, m * 256 + kk * 128: m * 256 + (kk + 1) * 128],
                                                       psTb[:, 0:128]), 1)

                with NB() as blk:
                    ev_done = {}
                    for wi, (wsrc, outf, outb) in enumerate(((tkT, zf, zb), (tqT, qf, qb))):
                        for m in range(8):
                            idx = wi * 8 + m
                            wb = wstr if idx % 2 == 0 else wstr2
                            wsm = s_wta if idx % 2 == 0 else s_wtb
                            g = (s_mm, ev_done[idx - 2][0]) if idx >= 2 else None
                            wsm, wv = stream_w(blk, wb[:], rearr(wsrc[:, m * 128:(m + 1) * 128]), guard=g, sem=wsm)

                            @blk.tensor
                            def _(e, m=m, idx=idx, wv=wv, wb=wb, wsm=wsm):
                                e.wait_ge(wsm.h, wv)
                                if idx >= 2:
                                    e.wait_ge(s_ac.h, ev_done[idx - 2][1])
                                pst = ps[idx % 2]
                                for kk in range(8):
                                    i = e.matmul(pst[:, 0:TB], wb[:, kk * 128:(kk + 1) * 128],
                                                 xn[:, kk * TC + c0: kk * TC + c0 + TB],
                                                 start=(kk == 0), stop=(kk == 7))
                                s_mm.inc(i, 1)
                            mv = s_mm.v

                            @blk.scalar
                            def _(e, m=m, idx=idx, mv=mv, outf=outf):
                                e.wait_ge(s_mm.h, mv)
                                s_ac.inc(e.copy(outf[:, m * 512:(m + 1) * 512], ps[idx % 2][:, 0:TB]), 1)
                            av = s_ac.v
                            ev_done[idx] = (mv, av)

                            @blk.vector
                            def _(e, m=m, av=av, outf=outf, outb=outb):
                                e.wait_ge(s_ac.h, av)
                                s_vv.inc(e.tensor_copy(outb[:, m * 512:(m + 1) * 512],
                                                       outf[:, m * 512:(m + 1) * 512]), 1)

                if DUMP and step == 0:
                    with NB() as blk:
                        @blk.sync
                        def _(e):
                            s_st.inc(e.dma_start(dbg1[:, :], qf[:]), 16)
                            s_st.inc(e.dma_start(dbg2[:, :], zf[:]), 16)
                            for kk in range(8):
                                s_st.inc(e.dma_start(
                                    dbg3[:, kk * 256:(kk + 1) * 256].bitcast(I32),
                                    xn[:, kk * TC: kk * TC + 512].bitcast(I32)), 16)
                            s_st.w(e)

                with NB() as blk:
                    for m in range(2):
                        @blk.tensor
                        def _(e, m=m):
                            for kk in range(8):
                                i = e.matmul(ps[m][:, 0:TB], w1tb[:, kk * 256 + m * 128: kk * 256 + (m + 1) * 128],
                                             zb[:, kk * 512:(kk + 1) * 512], start=(kk == 0), stop=(kk == 7))
                            s_mm.inc(i, 1)
                        mv = s_mm.v

                        @blk.scalar
                        def _(e, m=m, mv=mv):
                            e.wait_ge(s_mm.h, mv)
                            e.activation(af[:, m * 512:(m + 1) * 512], ps[m][:, 0:TB], AF.Identity,
                                         bias=b1c[:, m:m + 1], scale=1.0)
                            s_ac.inc(e.activation(hb[:, m * 512:(m + 1) * 512], ps[m][:, 0:TB], AF.Gelu,
                                                  bias=b1c[:, m:m + 1], scale=1.0), 1)

                with NB() as blk:
                    done = {}
                    for m in range(8):
                        wb = wstr if m % 2 == 0 else wstr2
                        wsm = s_wta if m % 2 == 0 else s_wtb
                        g = (s_vv, done[m - 2]) if m >= 2 else None
                        wsm, wv = stream_w(blk, wb[:], rearr(tvT[:, m * 128:(m + 1) * 128]), guard=g, sem=wsm)
                        pr = ps[2] if m % 2 == 0 else ps[4]
                        py = ps[3] if m % 2 == 0 else ps[5]

                        @blk.tensor
                        def _(e, m=m, wv=wv, wb=wb, pr=pr, py=py, wsm=wsm):
                            e.wait_ge(wsm.h, wv)
                            if m >= 2:
                                e.wait_ge(s_vv.h, done[m - 2])
                            for kk in range(2):
                                e.matmul(pr[:, 0:TB], w2tb[:, kk * 1024 + m * 128: kk * 1024 + (m + 1) * 128],
                                         hb[:, kk * 512:(kk + 1) * 512], start=(kk == 0), stop=(kk == 1))
                            for kk in range(8):
                                i = e.matmul(py[:, 0:TB], wb[:, kk * 128:(kk + 1) * 128],
                                             xn[:, kk * TC + c0: kk * TC + c0 + TB],
                                             start=(kk == 0), stop=(kk == 7))
                            s_mm.inc(i, 1)
                        mv = s_mm.v

                        @blk.scalar
                        def _(e, m=m, mv=mv, pr=pr):
                            e.wait_ge(s_mm.h, mv)
                            s_ac.inc(e.activation(sc32[:] if m % 2 == 0 else sc32b[:], pr[:, 0:TB],
                                                  AF.Identity, bias=b2c[:, m:m + 1], scale=1.0), 1)
                        av = s_ac.v

                        @blk.vector
                        def _(e, m=m, av=av, py=py):
                            e.wait_ge(s_ac.h, av)
                            sc = sc32 if m % 2 == 0 else sc32b
                            e.tensor_tensor(sc[:], sc[:], zf[:, m * 512:(m + 1) * 512], ALU.add)
                            s_vv.inc(e.tensor_tensor(grb[:, m * 512:(m + 1) * 512], sc[:], py[:, 0:TB], ALU.subtract), 1)
                            done[m] = s_vv.v

                with NB() as blk:
                    @blk.scalar
                    def _(e):
                        e.activation(gp1[:], af[:], AF.Square)
                        s_ac.inc(e.activation(gp1[:], gp1[:], AF.Exp, bias=0.0, scale=-0.5), 1)
                    av = s_ac.v

                    @blk.scalar
                    def _(e):
                        e.activation(gsgn[:], af[:], AF.Sign)
                        s_ac.inc(e.activation(gp2[:], af[:], AF.Abs), 1)
                    sgv = s_ac.v

                    @blk.vector
                    def _(e, av=av, sgv=sgv):
                        e.wait_ge(s_ac.h, max(av, sgv))
                        e.tensor_scalar(gp2[:], gp2[:], 1e-9, None, ALU.max)
                        s_vv.inc(e.reciprocal(gp2[:], gp2[:]), 1)
                        s_vv.w(e)
                        e.wait_ge(s_ac.h, sgv)
                        e.tensor_tensor(gp2[:], gp2[:], gsgn[:], ALU.mult)
                        e.tensor_tensor(gp2[:], hb[:], gp2[:], ALU.mult)
                        e.scalar_tensor_tensor(gp1[:], af[:], 0.3989422804014327, gp1[:], ALU.mult, ALU.mult)
                        s_vv.inc(e.tensor_tensor(gp2[:], gp2[:], gp1[:], ALU.add), 1)
                    vv = s_vv.v

                    for m in range(2):
                        @blk.tensor
                        def _(e, m=m, vv=vv):
                            e.wait_ge(s_vv.h, vv)
                            for kk in range(8):
                                i = e.matmul(ps[m][:, 0:TB], w2b[:, kk * 256 + m * 128: kk * 256 + (m + 1) * 128],
                                             grb[:, kk * 512:(kk + 1) * 512], start=(kk == 0), stop=(kk == 7))
                            s_mm.inc(i, 1)
                        mv = s_mm.v

                        @blk.vector
                        def _(e, m=m, mv=mv):
                            e.wait_ge(s_mm.h, mv)
                            s_vv.inc(e.tensor_tensor(gab[:, m * 512:(m + 1) * 512], ps[m][:, 0:TB],
                                                     gp2[:, m * 512:(m + 1) * 512], ALU.mult), 1)

                with NB() as blk:
                    for (src, dst, nkk) in ((zb, ztm, 8), (grb, grtm, 8), (hb, htm, 2), (gab, gatm, 2)):
                        for kk in range(nkk):
                            for t4 in range(4):
                                @blk.tensor
                                def _(e, src=src, kk=kk, t4=t4):
                                    e.wait_ge(s_vv.h, s_vv.v)
                                    s_mm.inc(e.transpose(psTb[:, 0:128],
                                                         src[:, kk * 512 + t4 * 128: kk * 512 + (t4 + 1) * 128],
                                                         ident[:]), 1)
                                mv = s_mm.v

                                @blk.vector
                                def _(e, dst=dst, nkk=nkk, kk=kk, t4=t4, mv=mv):
                                    e.wait_ge(s_mm.h, mv)
                                    w = nkk * 128
                                    s_vv.inc(e.tensor_copy(dst[:, t4 * w + kk * 128: t4 * w + (kk + 1) * 128],
                                                           psTb[:, 0:128]), 1)

                with NB() as blk:
                    for m in range(8):
                        @blk.tensor
                        def _(e, m=m):
                            if m >= 2:
                                e.wait_ge(s_vv.h, s_vv.v)
                            for t4 in range(4):
                                i = e.matmul(ps[m % 2][:, 0:256],
                                             ztm[:, t4 * 1024 + m * 128: t4 * 1024 + (m + 1) * 128],
                                             gatm[:, t4 * 256:(t4 + 1) * 256],
                                             start=(t4 == 0), stop=(t4 == 3))
                            s_mm.inc(i, 1)
                        mv = s_mm.v

                        @blk.vector
                        def _(e, m=m, mv=mv):
                            e.wait_ge(s_mm.h, mv)
                            s_vv.inc(e.tensor_copy(dw1[:, m * 256:(m + 1) * 256], ps[m % 2][:, 0:256]), 1)
                    for m in range(2):
                        for dh_ in range(2):
                            @blk.tensor
                            def _(e, m=m, dh_=dh_):
                                e.wait_ge(s_vv.h, s_vv.v)
                                for t4 in range(4):
                                    i = e.matmul(ps[2 + (2 * m + dh_) % 2][:, 0:512],
                                                 htm[:, t4 * 256 + m * 128: t4 * 256 + (m + 1) * 128],
                                                 grtm[:, t4 * 1024 + dh_ * 512: t4 * 1024 + (dh_ + 1) * 512],
                                                 start=(t4 == 0), stop=(t4 == 3))
                                s_mm.inc(i, 1)
                            mv = s_mm.v

                            @blk.vector
                            def _(e, m=m, dh_=dh_, mv=mv):
                                e.wait_ge(s_mm.h, mv)
                                s_vv.inc(e.tensor_copy(dw2[:, m * 1024 + dh_ * 512: m * 1024 + (dh_ + 1) * 512],
                                                       ps[2 + (2 * m + dh_) % 2][:, 0:512]), 1)

                    @blk.vector
                    def _(e):
                        for kk in range(8):
                            s_vv.inc(e.reduce_sum(dbs[:, kk:kk + 1], grb[:, kk * 512:(kk + 1) * 512], axis=AX.X), 1)
                        for m in range(2):
                            s_vv.inc(e.reduce_sum(dbs[:, 8 + m:9 + m], gab[:, m * 512:(m + 1) * 512], axis=AX.X), 1)

                with NB() as blk:
                    @blk.sync
                    def _(e):
                        e.wait_ge(s_vv.h, s_vv.v)
                        ariv = ar_in[:].rearrange("(p n) -> p n", p=128)
                        s_ld.inc(e.dma_start(ariv[:, 0:2048], dw1[:]), 16)
                        s_ld.inc(e.dma_start(ariv[:, 2048:4096], dw2[:]), 16)
                        s_ld.inc(e.dma_start(ariv[:, 4096:4106], dbs[:]), 16)
                    ldv = s_ld.v

                    @blk.gpsimd
                    def _(e, ldv=ldv):
                        e.wait_ge(s_ld.h, ldv)
                        s_cc.inc(e.collective_compute("AllReduce", ALU.add, RG, [ar_in[:]], [ar_out[:]]), 1)
                    ccv = s_cc.v

                    @blk.sync
                    def _(e, ccv=ccv):
                        e.wait_ge(s_cc.h, ccv)
                        arv = ar_out[:].rearrange("(p n) -> p n", p=128)
                        s_ld.inc(e.dma_start(dw1[:], arv[:, 0:2048]), 16)
                        s_ld.inc(e.dma_start(dw2[:], arv[:, 2048:4096]), 16)
                        s_ld.inc(e.dma_start(dbs[:], arv[:, 4096:4106]), 16)
                    ldv = s_ld.v

                    @blk.vector
                    def _(e, ldv=ldv):
                        e.wait_ge(s_ld.h, ldv)
                        e.scalar_tensor_tensor(w1t32[:], dw1[:], -LRC, w1t32[:], ALU.mult, ALU.add)
                        e.scalar_tensor_tensor(w2t32[:], dw2[:], -LRC, w2t32[:], ALU.mult, ALU.add)
                        e.scalar_tensor_tensor(b2c[:], dbs[:, 0:8], -LRC, b2c[:], ALU.mult, ALU.add)
                        e.scalar_tensor_tensor(b1c[:], dbs[:, 8:10], -LRC, b1c[:], ALU.mult, ALU.add)
                        for kk in range(8):
                            e.tensor_copy(w1tb[:, kk * 256:(kk + 1) * 256], w1t32[:, kk * 256:(kk + 1) * 256])
                        for kk in range(2):
                            s_vv.inc(e.tensor_copy(w2tb[:, kk * 1024:(kk + 1) * 1024],
                                                   w2t32[:, kk * 1024:(kk + 1) * 1024]), 1)

                with NB() as blk:
                    for m in range(2):
                        @blk.tensor
                        def _(e, m=m):
                            e.wait_ge(s_vv.h, s_vv.v)
                            for kk in range(8):
                                i = e.matmul(ps[m][:, 0:TB], w1tb[:, kk * 256 + m * 128: kk * 256 + (m + 1) * 128],
                                             qb[:, kk * 512:(kk + 1) * 512], start=(kk == 0), stop=(kk == 7))
                            s_mm.inc(i, 1)
                        mv = s_mm.v

                        @blk.scalar
                        def _(e, m=m, mv=mv):
                            e.wait_ge(s_mm.h, mv)
                            s_ac.inc(e.activation(hb[:, m * 512:(m + 1) * 512], ps[m][:, 0:TB], AF.Gelu,
                                                  bias=b1c[:, m:m + 1], scale=1.0), 1)
                    acv = s_ac.v

                    st_done = {}
                    xst_done = {}
                    for m in range(8):
                        xbuf = xs_a if m % 2 == 0 else xs_d
                        scb = sc32 if m % 2 == 0 else sc32b

                        xsm = s_lda if m % 2 == 0 else s_ldb

                        @blk.sync
                        def _(e, m=m, xbuf=xbuf, xsm=xsm):
                            if m >= 2:
                                e.wait_ge(s_vv.h, st_done[m - 2])
                            xsm.inc(e.dma_start(xbuf[:], xT[m * 128:(m + 1) * 128, c0:c0 + TB]), 16)
                        ldv = (xsm, xsm.v)

                        @blk.tensor
                        def _(e, m=m, acv=acv):
                            e.wait_ge(s_ac.h, acv)
                            if m >= 2:
                                e.wait_ge(s_vv.h, st_done[m - 2])
                            for kk in range(2):
                                i = e.matmul(ps[2 + m % 2][:, 0:TB],
                                             w2tb[:, kk * 1024 + m * 128: kk * 1024 + (m + 1) * 128],
                                             hb[:, kk * 512:(kk + 1) * 512], start=(kk == 0), stop=(kk == 1))
                            s_mm.inc(i, 1)
                        mv = s_mm.v

                        @blk.scalar
                        def _(e, m=m, mv=mv):
                            e.wait_ge(s_mm.h, mv)
                            s_ac.inc(e.activation(gp1[:, 0:512] if m % 2 == 0 else gp1[:, 512:1024],
                                                  ps[2 + m % 2][:, 0:TB], AF.Identity,
                                                  bias=b2c[:, m:m + 1], scale=1.0), 1)
                        av = s_ac.v

                        @blk.vector
                        def _(e, m=m, av=av, ldv=ldv, xbuf=xbuf, scb=scb):
                            e.wait_ge(s_ac.h, av)
                            e.wait_ge(ldv[0].h, ldv[1])
                            if m >= 2:
                                e.wait_ge(s_st.h, xst_done[m - 2])
                            g = gp1[:, 0:512] if m % 2 == 0 else gp1[:, 512:1024]
                            e.tensor_tensor(scb[:], g, qf[:, m * 512:(m + 1) * 512], ALU.add)
                            s_vv.inc(e.tensor_tensor(scb[:], scb[:], xbuf[:], ALU.add), 1)
                            st_done[m] = s_vv.v
                        vv = s_vv.v

                        @blk.sync
                        def _(e, m=m, vv=vv, scb=scb):
                            e.wait_ge(s_vv.h, vv)
                            s_st.inc(e.dma_start(x_hbm[m * 128:(m + 1) * 128, c0:c0 + TB], scb[:]), 16)
                            xst_done[m] = s_st.v

                    @blk.sync
                    def _(e):
                        s_st.w(e)

    if upto >= 3:
        ln_pass(x_hbm)

        with nc.sbuf_tensor("hfull", [128, 24 * TC], BF16) as hfull, \
             nc.sbuf_tensor("fgw", [128, 1024], BF16) as fgw, \
             nc.sbuf_tensor("fuw", [128, 1024], BF16) as fuw, \
             nc.sbuf_tensor("fgw2", [128, 1024], BF16) as fgw2, \
             nc.sbuf_tensor("fuw2", [128, 1024], BF16) as fuw2, \
             nc.sbuf_tensor("fsl", [128, 512], BF16) as fsl, \
             nc.sbuf_tensor("fdw", [128, 3072], BF16) as fdw, \
             nc.sbuf_tensor("fdw2", [128, 3072], BF16) as fdw2:

            with NB() as blk:
                hdone = {}
                for m in range(24):
                    g_b, u_b = (fgw, fuw) if m % 2 == 0 else (fgw2, fuw2)
                    wsm = s_wta if m % 2 == 0 else s_wtb
                    g = (s_mm, hdone[m - 2][0]) if m >= 2 else None
                    stream_w(blk, g_b[:], rearr(wgT[:, m * 128:(m + 1) * 128]), guard=g, sem=wsm)
                    wsm, wv = stream_w(blk, u_b[:], rearr(wuT[:, m * 128:(m + 1) * 128]), guard=g, sem=wsm)

                    for tt4 in range(4):
                        c0 = tt4 * TB

                        @blk.tensor
                        def _(e, m=m, c0=c0, wv=wv, g_b=g_b, u_b=u_b, wsm=wsm):
                            e.wait_ge(wsm.h, wv)
                            e.wait_ge(s_vv.h, s_vv.v)
                            for kk in range(8):
                                e.matmul(ps[0][:, 0:TB], g_b[:, kk * 128:(kk + 1) * 128],
                                         xn[:, kk * TC + c0: kk * TC + c0 + TB], start=(kk == 0), stop=(kk == 7))
                            for kk in range(8):
                                i = e.matmul(ps[1][:, 0:TB], u_b[:, kk * 128:(kk + 1) * 128],
                                             xn[:, kk * TC + c0: kk * TC + c0 + TB], start=(kk == 0), stop=(kk == 7))
                            s_mm.inc(i, 1)
                        mv = s_mm.v

                        @blk.scalar
                        def _(e, mv=mv):
                            e.wait_ge(s_mm.h, mv)
                            s_ac.inc(e.activation(fsl[:], ps[0][:, 0:TB], AF.Silu, bias=0.0, scale=QS), 1)
                        av = s_ac.v

                        @blk.vector
                        def _(e, m=m, c0=c0, av=av):
                            e.wait_ge(s_ac.h, av)
                            s_vv.inc(e.tensor_tensor(hfull[:, m * TC + c0: m * TC + c0 + TB], fsl[:],
                                                     ps[1][:, 0:TB], ALU.mult), 1)
                    hdone[m] = (s_mm.v, s_vv.v)

            with NB() as blk:
                ddone = {}
                fstd = {}
                for m in range(8):
                    d_b = fdw if m % 2 == 0 else fdw2
                    wsm = s_wta if m % 2 == 0 else s_wtb
                    g = (s_mm, ddone[m - 2][0]) if m >= 2 else None
                    wsm, wv = stream_w(blk, d_b[:], rearr(wdT[:, m * 128:(m + 1) * 128]), guard=g, sem=wsm)
                    for tt4 in range(4):
                        c0 = tt4 * TB
                        xbuf = xs_a if tt4 % 2 == 0 else xs_d
                        xsm = s_lda if tt4 % 2 == 0 else s_ldb

                        @blk.sync
                        def _(e, m=m, c0=c0, xbuf=xbuf, xsm=xsm):
                            e.wait_ge(s_vv.h, max(0, s_vv.v - 1))
                            xsm.inc(e.dma_start(xbuf[:], x_hbm[m * 128:(m + 1) * 128, c0:c0 + TB]), 16)
                        ldv = (xsm, xsm.v)

                        @blk.tensor
                        def _(e, m=m, c0=c0, wv=wv, d_b=d_b, wsm=wsm):
                            e.wait_ge(wsm.h, wv)
                            e.wait_ge(s_vv.h, s_vv.v)
                            for kk in range(24):
                                i = e.matmul(ps[2][:, 0:TB], d_b[:, kk * 128:(kk + 1) * 128],
                                             hfull[:, kk * TC + c0: kk * TC + c0 + TB],
                                             start=(kk == 0), stop=(kk == 23))
                            s_mm.inc(i, 1)
                        mv = s_mm.v

                        scb = sc32 if tt4 % 2 == 0 else sc32b

                        @blk.vector
                        def _(e, mv=mv, ldv=ldv, xbuf=xbuf, scb=scb, m=m, tt4=tt4):
                            e.wait_ge(s_mm.h, mv)
                            e.wait_ge(ldv[0].h, ldv[1])
                            key = (m, tt4 - 2) if tt4 >= 2 else (m - 1, tt4 + 2)
                            if key in fstd:
                                e.wait_ge(s_st.h, fstd[key])
                            s_vv.inc(e.scalar_tensor_tensor(scb[:], ps[2][:, 0:TB], QS * QS, xbuf[:],
                                                            ALU.mult, ALU.add), 1)
                        vv = s_vv.v

                        @blk.sync
                        def _(e, m=m, c0=c0, vv=vv, scb=scb, tt4=tt4):
                            e.wait_ge(s_vv.h, vv)
                            s_st.inc(e.dma_start(x_hbm[m * 128:(m + 1) * 128, c0:c0 + TB], scb[:]), 16)
                            fstd[(m, tt4)] = s_st.v
                    ddone[m] = (s_mm.v, 0)

                @blk.sync
                def _(e):
                    s_st.w(e)

    if upto >= 4:
        ln_pass(x_hbm)

        with nc.sbuf_tensor("aw", [128, 1024], BF16) as aw, \
             nc.sbuf_tensor("aw2", [128, 1024], BF16) as aw2, \
             nc.sbuf_tensor("qkvt", [128, TC], BF16) as qkvt:
            with NB() as blk:
                qdone = {}
                for m in range(24):
                    a_b = aw if m % 2 == 0 else aw2
                    wsm = s_wta if m % 2 == 0 else s_wtb
                    g = (s_mm, qdone[m - 2][0]) if m >= 2 else None
                    wsm, wv = stream_w(blk, a_b[:], rearr(wattn[:, m * 128:(m + 1) * 128]), guard=g, sem=wsm)
                    for tt4 in range(4):
                        c0 = tt4 * TB

                        @blk.tensor
                        def _(e, m=m, c0=c0, wv=wv, a_b=a_b, tt4=tt4, wsm=wsm):
                            e.wait_ge(wsm.h, wv)
                            e.wait_ge(s_ac.h, max(0, s_ac.v - 1))
                            for kk in range(8):
                                i = e.matmul(ps[tt4 % 2][:, 0:TB], a_b[:, kk * 128:(kk + 1) * 128],
                                             xn[:, kk * TC + c0: kk * TC + c0 + TB], start=(kk == 0), stop=(kk == 7))
                            s_mm.inc(i, 1)
                        mv = s_mm.v

                        @blk.scalar
                        def _(e, m=m, c0=c0, mv=mv, tt4=tt4):
                            e.wait_ge(s_mm.h, mv)
                            if c0 == 0:
                                e.wait_ge(s_st.h, s_st.v)
                            s_ac.inc(e.activation(qkvt[:, c0:c0 + TB], ps[tt4 % 2][:, 0:TB], AF.Identity,
                                                  bias=battnc[:, m:m + 1], scale=1.0), 1)
                    av = s_ac.v
                    qdone[m] = (s_mm.v, av)

                    @blk.sync
                    def _(e, m=m, av=av):
                        e.wait_ge(s_ac.h, av)
                        s_st.inc(e.dma_start(ag1_in[m * 128:(m + 1) * 128, :], qkvt[:].bitcast(I32)), 16)
                stv = s_st.v

                @blk.gpsimd
                def _(e, stv=stv):
                    e.wait_ge(s_st.h, stv)
                    s_cc.inc(e.collective_compute("AllGather", ALU.bypass, RG, [ag1_in[:]], [ag1_out[:]]), 1)

    if upto >= 5:
        with ExitStack() as _st:
            _e = _st.enter_context
            qsl = _e(nc.sbuf_tensor("qsl", [128, TFULL], BF16))
            ksl = _e(nc.sbuf_tensor("ksl", [128, TFULL], BF16))
            vsl = _e(nc.sbuf_tensor("vsl", [128, TFULL], BF16))
            vtm = _e(nc.sbuf_tensor("vtm", [128, TFULL], BF16))
            p1 = _e(nc.sbuf_tensor("p1", [128, TFULL], BF16))
            p2 = _e(nc.sbuf_tensor("p2", [128, TFULL], BF16))
            pt1 = _e(nc.sbuf_tensor("pt1", [128, 512], BF16))
            pt2 = _e(nc.sbuf_tensor("pt2", [128, 512], BF16))
            asl = _e(nc.sbuf_tensor("asl", [128, 1024], F32))
            lsum = _e(nc.sbuf_tensor("lsum", [128, 20], F32))
            li1 = _e(nc.sbuf_tensor("li1", [128, 32], F32))
            li2 = _e(nc.sbuf_tensor("li2", [128, 32], F32))
            lt = _e(nc.sbuf_tensor("lt", [32, 128], F32))
            negm = _e(nc.sbuf_tensor("negm", [128, 1], F32))
            redc = _e(nc.sbuf_tensor("redc", [128, 2], F32))
            redr = _e(nc.sbuf_tensor("redr", [1, 128], F32))
            omine = _e(nc.sbuf_tensor("omine", [128, B * TFULL], BF16))
            psTb = _e(nc.psum_tensor("psTb4", [128, 512], BF16))
            psTf = _e(nc.psum_tensor("psTf4", [128, 128], F32))

            with NB() as blk:
                @blk.sync
                def _(e):
                    me = e.partition_id()
                    offa = e.snap(me * 256, min_val=0, max_val=7 * 256)
                    s_ld.inc(e.dma_start(asl[:, 0:512], a_hbm[bass.ds(offa, 128), :]), 16)
                    av2 = a_hbm[128:, :]
                    s_ld.inc(e.dma_start(asl[:, 512:1024], av2[bass.ds(offa, 128), :]), 16)
                    s_ld.inc(e.dma_start(sc32[:], mmask[:]), 16)
                ldv = s_ld.v

                @blk.sync
                def _(e):
                    e.wait_ge(s_cc.h, s_cc.v)
                    me = e.partition_id()
                    off = e.snap(me * 128, min_val=0, max_val=7 * 128)
                    for r in range(NCORES):
                        src = ag1_out[r * 3072:(r + 1) * 3072, :].rearrange("(sel q) c -> sel q c", q=1024)
                        dst = myqkv[:, r * 1024:(r + 1) * 1024].rearrange("(sel p) c -> sel p c", p=128)
                        s_wt.inc(e.dma_start(dst, src[:, bass.ds(off, 128), :]), 16)
                    s_wt.w(e)

                @blk.vector
                def _(e, ldv=ldv):
                    e.wait_ge(s_ld.h, ldv)
                    e.tensor_tensor(asl[:, 0:512], asl[:, 0:512], sc32[:], ALU.add)
                    s_vv.inc(e.tensor_tensor(asl[:, 512:1024], asl[:, 512:1024], sc32[:], ALU.add), 1)

            for b in range(B):
                with NB() as blk:
                    @blk.sync
                    def _(e, b=b):
                        e.wait_ge(s_wt.h, s_wt.v)
                        for (dst, sel) in ((qsl, 0), (ksl, 1), (vsl, 2)):
                            for r in range(NCORES):
                                s_ld.inc(e.dma_start(
                                    dst[:, r * TB:(r + 1) * TB].bitcast(I32),
                                    myqkv[sel * 128:(sel + 1) * 128,
                                          r * 1024 + b * 256: r * 1024 + (b + 1) * 256]), 16)
                    ldv = s_ld.v

                    for sj in range(32):
                        @blk.tensor
                        def _(e, sj=sj, ldv=ldv):
                            if sj == 0:
                                e.wait_ge(s_ld.h, ldv)
                            else:
                                e.wait_ge(s_vv.h, s_vv.v)
                            s_mm.inc(e.transpose(psTb[:, 0:128], vsl[:, sj * 128:(sj + 1) * 128], ident[:]), 1)
                        mv = s_mm.v

                        @blk.vector
                        def _(e, sj=sj, mv=mv):
                            e.wait_ge(s_mm.h, mv)
                            s_vv.inc(e.tensor_copy(vtm[:, sj * 128:(sj + 1) * 128], psTb[:, 0:128]), 1)

                    @blk.vector
                    def _(e, ldv=ldv):
                        e.wait_ge(s_ld.h, ldv)
                        s_vv.inc(e.reduce_max(redc[:, 0:1], qsl[:], axis=AX.X, apply_absolute_value=True), 1)
                        s_vv.inc(e.reduce_max(redc[:, 1:2], ksl[:], axis=AX.X, apply_absolute_value=True), 1)
                    vv = s_vv.v

                    for col in range(2):
                        @blk.tensor
                        def _(e, col=col, vv=vv):
                            e.wait_ge(s_vv.h, s_vv.v)
                            s_mm.inc(e.transpose(psTf[0:1, 0:128], redc[:, col:col + 1], identf[:]), 1)
                        mv = s_mm.v

                        @blk.vector
                        def _(e, col=col, mv=mv):
                            e.wait_ge(s_mm.h, mv)
                            s_vv.inc(e.tensor_copy(redr[:], psTf[0:1, 0:128]), 1)
                            s_vv.w(e)
                            s_vv.inc(e.reduce_max(lsum[0:1, 18 + col:19 + col], redr[:], axis=AX.X), 1)
                            s_vv.w(e)

                    @blk.vector
                    def _(e):
                        e.tensor_tensor(lsum[0:1, 18:19], lsum[0:1, 18:19], lsum[0:1, 19:20], ALU.mult)
                        s_vv.inc(e.tensor_scalar(lsum[0:1, 18:19], lsum[0:1, 18:19], -8.0, None, ALU.mult), 1)
                    vv = s_vv.v

                    @blk.sync
                    def _(e, vv=vv):
                        e.wait_ge(s_vv.h, vv)
                        s_bc.inc(e.dma_start(sc_hbm[0:1, 0:1], lsum[0:1, 18:19]), 16)
                        s_bc.w(e)
                        s_bc.inc(e.dma_start(negm[:], sc_hbm[0:1, 0:1].broadcast_to((128, 1))), 16)
                    ldnm = s_bc.v

                    exp_done = {}
                    red_done = {}
                    oevac = 0
                    for rb in range(32):
                        t0 = rb * 128
                        if t0 < 384:
                            chunks = [(0, t0 + 128, 384 - t0)]
                        else:
                            nfar = (t0 - 384) // 512
                            rem = (t0 - 384) - nfar * 512
                            chunks = [(ci * 512, 512, None) for ci in range(nfar)]
                            if rem:
                                chunks.append((nfar * 512, rem, None))
                            chunks.append((t0 - 384, 512, 0))
                        nch_ = len(chunks)

                        for ci, (sst, wid, aoff) in enumerate(chunks):
                            pA = ps[2 * (ci % 2)]
                            pB = ps[2 * (ci % 2) + 1]
                            key = (rb, ci - 2)

                            @blk.tensor
                            def _(e, t0=t0, sst=sst, wid=wid, pA=pA, pB=pB, ci=ci, rb=rb, ldnm=ldnm):
                                if rb == 0 and ci == 0:
                                    e.wait_ge(s_bc.h, ldnm)
                                if ci >= 2:
                                    e.wait_ge(s_ac.h, exp_done[(rb, ci - 2)])
                                elif rb >= 1:
                                    e.wait_ge(s_ac.h, s_ac.v)
                                if ci == 0 and rb >= 1:
                                    e.wait_ge(s_vv.h, red_done[rb - 1])
                                e.matmul(pA[:, 0:wid], qsl[0:64, t0:t0 + 128], ksl[0:64, sst:sst + wid],
                                         start=True, stop=True, tile_position=(0, 0))
                                s_mm.inc(e.matmul(pB[:, 0:wid], qsl[64:128, t0:t0 + 128], ksl[64:128, sst:sst + wid],
                                                  start=True, stop=True, tile_position=(64, 0)), 1)
                            mv = s_mm.v

                            vv = None
                            if aoff is not None:
                                @blk.vector
                                def _(e, pA=pA, pB=pB, wid=wid, aoff=aoff, mv=mv):
                                    e.wait_ge(s_mm.h, mv)
                                    e.tensor_tensor(pA[:, 0:wid], pA[:, 0:wid], asl[:, aoff:aoff + wid], ALU.add)
                                    s_vv.inc(e.tensor_tensor(pB[:, 0:wid], pB[:, 0:wid],
                                                             asl[:, 512 + aoff:512 + aoff + wid], ALU.add), 1)
                                vv = s_vv.v

                            @blk.scalar
                            def _(e, pA=pA, pB=pB, sst=sst, wid=wid, ci=ci, mv=mv, vv=vv):
                                if vv is not None:
                                    e.wait_ge(s_vv.h, vv)
                                else:
                                    e.wait_ge(s_mm.h, mv)
                                e.activation(p1[:, sst:sst + wid], pA[:, 0:wid], AF.Exp, bias=negm[:],
                                             scale=0.125, accum_out=lsum[:, ci:ci + 1])
                                s_ac.inc(e.activation(p2[:, sst:sst + wid], pB[:, 0:wid], AF.Exp, bias=negm[:],
                                                      scale=0.125, accum_out=lsum[:, 9 + ci:10 + ci]), 1)
                            exp_done[(rb, ci)] = s_ac.v
                        acv = s_ac.v

                        @blk.vector
                        def _(e, rb=rb, nch_=nch_, acv=acv):
                            e.wait_ge(s_ac.h, acv)
                            s_vv.inc(e.reduce_sum(redc[:, 0:1], lsum[:, 0:nch_], axis=AX.X), 1)
                            s_vv.inc(e.reduce_sum(redc[:, 1:2], lsum[:, 9:9 + nch_], axis=AX.X), 1)
                            s_vv.w(e)
                            s_vv.inc(e.reciprocal(li1[:, rb:rb + 1], redc[:, 0:1]), 1)
                            s_vv.inc(e.reciprocal(li2[:, rb:rb + 1], redc[:, 1:2]), 1)
                            red_done[rb] = s_vv.v

                        nsj = rb + 1
                        for sj4 in range(0, nsj, 4):
                            hi = min(sj4 + 4, nsj)

                            @blk.tensor
                            def _(e, sj4=sj4, hi=hi, acv=acv, oevac=oevac):
                                e.wait_ge(s_ac.h, acv)
                                e.wait_ge(s_vv.h, s_vv.v)
                                if sj4 == 0:
                                    e.wait_ge(s_ac.h, max(acv, oevac))
                                for sj in range(sj4, hi):
                                    i = e.transpose(psTb[:, (sj - sj4) * 128:(sj - sj4 + 1) * 128],
                                                    p1[:, sj * 128:(sj + 1) * 128], ident[:])
                                s_mm.inc(i, 1)
                            mv = s_mm.v

                            @blk.vector
                            def _(e, sj4=sj4, hi=hi, mv=mv):
                                e.wait_ge(s_mm.h, mv)
                                s_vv.inc(e.tensor_copy(pt1[:, 0:(hi - sj4) * 128], psTb[:, 0:(hi - sj4) * 128]), 1)
                            vv = s_vv.v

                            @blk.tensor
                            def _(e, sj4=sj4, hi=hi, vv=vv):
                                e.wait_ge(s_vv.h, vv)
                                for sj in range(sj4, hi):
                                    i = e.transpose(psTb[:, (sj - sj4) * 128:(sj - sj4 + 1) * 128],
                                                    p2[:, sj * 128:(sj + 1) * 128], ident[:])
                                s_mm.inc(i, 1)
                            mv = s_mm.v

                            @blk.vector
                            def _(e, sj4=sj4, hi=hi, mv=mv):
                                e.wait_ge(s_mm.h, mv)
                                s_vv.inc(e.tensor_copy(pt2[:, 0:(hi - sj4) * 128], psTb[:, 0:(hi - sj4) * 128]), 1)
                            vv = s_vv.v

                            @blk.tensor
                            def _(e, sj4=sj4, hi=hi, nsj=nsj, vv=vv):
                                e.wait_ge(s_vv.h, vv)
                                for sj in range(sj4, hi):
                                    e.matmul(ps[4][0:64, 0:128], vtm[:, sj * 128: sj * 128 + 64],
                                             pt1[:, (sj - sj4) * 128:(sj - sj4 + 1) * 128],
                                             start=(sj == 0), stop=(sj == nsj - 1), tile_position=(0, 0))
                                    i = e.matmul(ps[5][64:128, 0:128], vtm[:, sj * 128 + 64:(sj + 1) * 128],
                                                 pt2[:, (sj - sj4) * 128:(sj - sj4 + 1) * 128],
                                                 start=(sj == 0), stop=(sj == nsj - 1), tile_position=(0, 64))
                                s_mm.inc(i, 1)
                        mv = s_mm.v

                        @blk.scalar
                        def _(e, b=b, t0=t0, mv=mv):
                            e.wait_ge(s_mm.h, mv)
                            e.copy(omine[0:64, b * TFULL + t0: b * TFULL + t0 + 128], ps[4][0:64, 0:128])
                            s_ac.inc(e.copy(omine[64:128, b * TFULL + t0: b * TFULL + t0 + 128],
                                            ps[5][64:128, 0:128]), 1)
                        oevac = s_ac.v

                    vvb = s_vv.v
                    for hh, lic in ((0, li1), (1, li2)):
                        @blk.tensor
                        def _(e, lic=lic, vvb=vvb, hh=hh):
                            e.wait_ge(s_vv.h, s_vv.v)
                            s_mm.inc(e.transpose(psTf[0:32, 0:128], lic[:], identf[:]), 1)
                        mv = s_mm.v

                        @blk.vector
                        def _(e, mv=mv, hh=hh):
                            e.wait_ge(s_mm.h, mv)
                            if hh == 1:
                                e.wait_ge(s_st.h, s_st.v)
                            s_vv.inc(e.tensor_copy(lt[:], psTf[0:32, 0:128]), 1)
                        vv = s_vv.v

                        @blk.sync
                        def _(e, b=b, hh=hh, vv=vv):
                            e.wait_ge(s_vv.h, vv)
                            s_st.inc(e.dma_start(
                                ag3_in[hh:hh + 1, b * TFULL:(b + 1) * TFULL].rearrange(
                                    "o (a c) -> (o a) c", c=128), lt[:]), 16)

            with NB() as blk:
                @blk.sync
                def _(e):
                    s_st.inc(e.dma_start(ag2_in[:, :], omine[:].bitcast(I32)), 16)
                ldv = s_st.v

                @blk.gpsimd
                def _(e, ldv=ldv):
                    e.wait_ge(s_st.h, s_st.v)
                    s_cc.inc(e.collective_compute("AllGather", ALU.bypass, RG, [ag2_in[:]], [ag2_out[:]]), 1)
                    s_cc.inc(e.collective_compute("AllGather", ALU.bypass, RG, [ag3_in[:]], [ag3_out[:]]), 1)

    if upto >= 6:
        with nc.sbuf_tensor("myl_sb", [16, TC], F32) as myl_sb, \
             nc.sbuf_tensor("ilb", [128, 512], F32) as ilb, \
             nc.sbuf_tensor("pw", [128, 1024], BF16) as pw, \
             nc.sbuf_tensor("pw2", [128, 1024], BF16) as pw2, \
             nc.sbuf_tensor("ynorm", [128, 8 * TC], BF16) as ynorm:

            with NB() as blk:
                @blk.scalar
                def _(e):
                    e.wait_ge(s_cc.h, s_cc.v)
                    me = e.partition_id()
                    off256 = e.snap(me * 256, min_val=0, max_val=7 * 256)
                    off512 = e.snap(me * 512, min_val=0, max_val=7 * 512)
                    for r in range(NCORES):
                        src = ag2_out[r * 128:(r + 1) * 128, :].rearrange(
                            "p (bb c) -> p bb c", c=2048)
                        dst = ynorm[:, r * TC:(r + 1) * TC].bitcast(I32).rearrange(
                            "p (bb c) -> p bb c", c=256)
                        s_sg.inc(e.dma_start(dst, src[:, :, bass.ds(off256, 256)]), 16)
                    src3 = ag3_out[:, :].rearrange("h (bb c) -> h bb c", c=TFULL)
                    s_sg.inc(e.dma_start(
                        myl_sb[:, :].rearrange("h (bb c) -> h bb c", c=TB),
                        src3[:, :, bass.ds(off512, 512)]), 16)
                    s_sg.w(e)
                    s_sg.inc(e.dma_start(myl[:, :], myl_sb[:]), 16)
                ldall = s_sg.v

                ndone = {}
                for r in range(NCORES):
                    for bb in range(B):
                        @blk.sync
                        def _(e, r=r, bb=bb, ldall=ldall):
                            if r == 0 and bb == 0:
                                s_sg.w(e, ldall)
                            if not (r == 0 and bb == 0):
                                e.wait_ge(s_vv.h, ndone[(r, bb - 1) if bb else (r - 1, B - 1)])
                            for hh in range(2):
                                s_ld.inc(e.dma_start(
                                    ilb[hh * 64:(hh + 1) * 64, :],
                                    myl[2 * r + hh: 2 * r + hh + 1,
                                        bb * TB:(bb + 1) * TB].broadcast_to((64, TB))), 16)
                        ldv2 = s_ld.v

                        @blk.vector
                        def _(e, r=r, bb=bb, ldv2=ldv2, ldall=ldall):
                            e.wait_ge(s_sg.h, ldall)
                            e.wait_ge(s_ld.h, ldv2)
                            sl = ynorm[:, r * TC + bb * TB: r * TC + (bb + 1) * TB]
                            s_vv.inc(e.tensor_tensor(sl, sl, ilb[:], ALU.mult), 1)
                            ndone[(r, bb)] = s_vv.v

            with NB() as blk:
                pdone = {}
                pstd = {}
                for m in range(8):
                    p_b = pw if m % 2 == 0 else pw2
                    wsm = s_wta if m % 2 == 0 else s_wtb
                    g = (s_mm, pdone[m - 2][0]) if m >= 2 else None
                    wsm, wv = stream_w(blk, p_b[:], rearr(wproj[:, m * 128:(m + 1) * 128]), guard=g, sem=wsm)
                    for tt4 in range(4):
                        c0 = tt4 * TB
                        xbuf = xs_a if tt4 % 2 == 0 else xs_d
                        xsm = s_lda if tt4 % 2 == 0 else s_ldb

                        @blk.sync
                        def _(e, m=m, c0=c0, xbuf=xbuf, xsm=xsm):
                            e.wait_ge(s_vv.h, max(0, s_vv.v - 1))
                            xsm.inc(e.dma_start(xbuf[:], x_hbm[m * 128:(m + 1) * 128, c0:c0 + TB]), 16)
                        ldv = (xsm, xsm.v)

                        @blk.tensor
                        def _(e, m=m, c0=c0, wv=wv, p_b=p_b, wsm=wsm):
                            e.wait_ge(wsm.h, wv)
                            e.wait_ge(s_ac.h, s_ac.v)
                            for kk in range(8):
                                i = e.matmul(ps[0][:, 0:TB], p_b[:, kk * 128:(kk + 1) * 128],
                                             ynorm[:, kk * TC + c0: kk * TC + c0 + TB], start=(kk == 0), stop=(kk == 7))
                            s_mm.inc(i, 1)
                        mv = s_mm.v

                        @blk.scalar
                        def _(e, m=m, mv=mv):
                            e.wait_ge(s_mm.h, mv)
                            e.wait_ge(s_vv.h, s_vv.v)
                            s_ac.inc(e.activation(sc32b[:], ps[0][:, 0:TB], AF.Identity,
                                                  bias=bprojc[:, m:m + 1], scale=1.0), 1)
                        av = s_ac.v

                        scb = mb if tt4 % 2 == 0 else ib

                        @blk.vector
                        def _(e, av=av, ldv=ldv, xbuf=xbuf, scb=scb, m=m, tt4=tt4):
                            e.wait_ge(s_ac.h, av)
                            e.wait_ge(ldv[0].h, ldv[1])
                            key = (m, tt4 - 2) if tt4 >= 2 else (m - 1, tt4 + 2)
                            if key in pstd:
                                e.wait_ge(s_st.h, pstd[key])
                            s_vv.inc(e.tensor_tensor(scb[:], sc32b[:], xbuf[:], ALU.add), 1)
                        vv = s_vv.v

                        @blk.sync
                        def _(e, m=m, c0=c0, vv=vv, scb=scb, tt4=tt4):
                            e.wait_ge(s_vv.h, vv)
                            s_st.inc(e.dma_start(x_hbm[m * 128:(m + 1) * 128, c0:c0 + TB], scb[:]), 16)
                            pstd[(m, tt4)] = s_st.v
                    pdone[m] = (s_mm.v, 0)

                @blk.sync
                def _(e):
                    s_st.w(e)

    if upto >= 7:
        ln_pass(x_hbm)

        with nc.sbuf_tensor("rw_sb", [128, 128], BF16) as rw_sb, \
             nc.sbuf_tensor("lgf", [16, TC], F32) as lgf, \
             nc.sbuf_tensor("lgt", [128, 16], F32) as lgt, \
             nc.sbuf_tensor("top", [128, 8], F32) as top, \
             nc.sbuf_tensor("e1", [128, 16], F32) as e1, \
             nc.sbuf_tensor("e2", [128, 16], F32) as e2, \
             nc.sbuf_tensor("cmbf", [16, TC], F32) as cmbf, \
             nc.psum_tensor("psTf", [128, 128], F32) as psTf:
            with NB() as blk:
                wsm, wv = stream_w(blk, rw_sb[:], rearr(rwT[:]), sem=s_wta)
                for tt4 in range(4):
                    c0 = tt4 * TB

                    @blk.tensor
                    def _(e, c0=c0, wv=wv, tt4=tt4, wsm=wsm):
                        e.wait_ge(wsm.h, wv)
                        e.wait_ge(s_ac.h, s_ac.v)
                        for kk in range(8):
                            i = e.matmul(ps[0][0:16, 0:TB], rw_sb[:, kk * 16:(kk + 1) * 16],
                                         xn[:, kk * TC + c0: kk * TC + c0 + TB], start=(kk == 0), stop=(kk == 7))
                        s_mm.inc(i, 1)
                    mv = s_mm.v

                    @blk.scalar
                    def _(e, c0=c0, mv=mv):
                        e.wait_ge(s_mm.h, mv)
                        s_ac.inc(e.activation(lgf[:, c0:c0 + TB], ps[0][0:16, 0:TB], AF.Identity,
                                              bias=rbc[:], scale=1.0), 1)
                acv = s_ac.v

                for ti in range(16):
                    @blk.tensor
                    def _(e, ti=ti, acv=acv):
                        e.wait_ge(s_ac.h, acv)
                        e.wait_ge(s_vv.h, s_vv.v)
                        s_mm.inc(e.transpose(psTf[0:128, 0:16], lgf[:, ti * 128:(ti + 1) * 128], identf[0:16, 0:16]), 1)
                    mv = s_mm.v

                    @blk.vector
                    def _(e, ti=ti, mv=mv):
                        e.wait_ge(s_mm.h, mv)
                        s_vv.inc(e.tensor_copy(lgt[:], psTf[0:128, 0:16]), 1)
                        s_vv.w(e)
                        s_vv.inc(e.reduce_max(top[:, 0:1], lgt[:], axis=AX.X), 1)
                        s_vv.w(e)
                        e.tensor_tensor(e1[:], lgt[:], top[:, 0:1].broadcast_to((128, 16)), ALU.is_ge)
                        s_vv.inc(e.scalar_tensor_tensor(e2[:], e1[:], NEG, lgt[:], ALU.mult, ALU.add), 1)
                        s_vv.w(e)
                        s_vv.inc(e.reduce_max(top[:, 1:2], e2[:], axis=AX.X), 1)
                        s_vv.w(e)
                        e.tensor_tensor(e2[:], e2[:], top[:, 1:2].broadcast_to((128, 16)), ALU.is_ge)
                        s_vv.inc(e.tensor_tensor(top[:, 2:3], top[:, 1:2], top[:, 0:1], ALU.subtract), 1)
                    vv = s_vv.v

                    @blk.scalar
                    def _(e, vv=vv):
                        e.wait_ge(s_vv.h, vv)
                        e.activation(top[:, 3:4], top[:, 2:3], AF.Sigmoid)
                        s_ac.inc(e.activation(top[:, 4:5], top[:, 2:3], AF.Sigmoid, bias=0.0, scale=-1.0), 1)
                    av = s_ac.v

                    @blk.vector
                    def _(e, av=av):
                        e.wait_ge(s_ac.h, av)
                        e.tensor_tensor(e1[:], e1[:], top[:, 4:5].broadcast_to((128, 16)), ALU.mult)
                        e.tensor_tensor(e2[:], e2[:], top[:, 3:4].broadcast_to((128, 16)), ALU.mult)
                        s_vv.inc(e.tensor_tensor(lgt[:], e1[:], e2[:], ALU.add), 1)
                    vv = s_vv.v

                    @blk.tensor
                    def _(e, ti=ti, vv=vv):
                        e.wait_ge(s_vv.h, vv)
                        s_mm.inc(e.transpose(psTf[0:16, 0:128], lgt[:], identf[:]), 1)
                    mv = s_mm.v

                    @blk.vector
                    def _(e, ti=ti, mv=mv):
                        e.wait_ge(s_mm.h, mv)
                        s_vv.inc(e.tensor_copy(cmbf[:, ti * 128:(ti + 1) * 128], psTf[0:16, 0:128]), 1)
                vv = s_vv.v

                @blk.sync
                def _(e, vv=vv):
                    e.wait_ge(s_vv.h, vv)
                    s_st.inc(e.dma_start(cmb_hbm[:, :], cmbf[:]), 16)
                    s_st.w(e)

        with nc.sbuf_tensor("hall", [128, 72 * TB], BF16) as hall, \
             nc.sbuf_tensor("egw", [128, 4096], BF16) as egw, \
             nc.sbuf_tensor("euw", [128, 4096], BF16) as euw, \
             nc.sbuf_tensor("egw2", [128, 4096], BF16) as egw2, \
             nc.sbuf_tensor("euw2", [128, 4096], BF16) as euw2, \
             nc.sbuf_tensor("edw", [128, 4096], BF16) as edw, \
             nc.sbuf_tensor("edw2", [128, 4096], BF16) as edw2, \
             nc.sbuf_tensor("cbb", [128, 512], F32) as cbb, \
             nc.sbuf_tensor("fsl2", [128, 512], BF16) as fsl2, \
             nc.psum_tensor("pd6", [128, 512], F32) as pd6, \
             nc.psum_tensor("pd7", [128, 512], F32) as pd7:

            pall = ps + [pd6, pd7]
            mstd = {}
            for tt4 in range(4):
                c0 = tt4 * TB

                with NB() as blk:
                    mm_done = {}
                    for e_i in range(18):
                        g_b, u_b = (egw, euw) if e_i % 2 == 0 else (egw2, euw2)
                        wsm = s_wta if e_i % 2 == 0 else s_wtb
                        gd = (s_mm, mm_done[e_i - 2]) if e_i >= 2 else None
                        if e_i < 16:
                            stream_w(blk, g_b[:], rearr(gate_p[e_i]), guard=gd, sem=wsm)
                            wsm, wv = stream_w(blk, u_b[:], rearr(up_p[e_i]), guard=gd, sem=wsm)
                        else:
                            half = e_i - 16
                            stream_w(blk, g_b[:], rearr(shgT[:, half * 512:(half + 1) * 512]), guard=gd, sem=wsm)
                            wsm, wv = stream_w(blk, u_b[:], rearr(shuT[:, half * 512:(half + 1) * 512]), guard=gd, sem=wsm)

                        ldv = 0
                        if e_i < 16:
                            @blk.sync
                            def _(e, e_i=e_i, c0=c0):
                                e.wait_ge(s_vv.h, s_vv.v)
                                s_ld.inc(e.dma_start(cbb[:], cmb_hbm[e_i:e_i + 1, c0:c0 + TB].broadcast_to((128, TB))), 16)
                            ldv = s_ld.v

                        for m in range(4):
                            hc = e_i * 4 + m if e_i < 16 else 64 + (e_i - 16) * 4 + m

                            @blk.tensor
                            def _(e, m=m, c0=c0, wv=wv, g_b=g_b, u_b=u_b, wsm=wsm):
                                e.wait_ge(wsm.h, wv)
                                e.wait_ge(s_vv.h, s_vv.v)
                                for kk in range(8):
                                    e.matmul(ps[0][:, 0:TB], g_b[:, kk * 512 + m * 128: kk * 512 + (m + 1) * 128],
                                             xn[:, kk * TC + c0: kk * TC + c0 + TB], start=(kk == 0), stop=(kk == 7))
                                for kk in range(8):
                                    i = e.matmul(ps[1][:, 0:TB], u_b[:, kk * 512 + m * 128: kk * 512 + (m + 1) * 128],
                                                 xn[:, kk * TC + c0: kk * TC + c0 + TB], start=(kk == 0), stop=(kk == 7))
                                s_mm.inc(i, 1)
                            mv = s_mm.v
                            mm_done[e_i] = mv

                            @blk.scalar
                            def _(e, mv=mv):
                                e.wait_ge(s_mm.h, mv)
                                s_ac.inc(e.activation(fsl2[:], ps[0][:, 0:TB], AF.Silu), 1)
                            av = s_ac.v

                            @blk.vector
                            def _(e, hc=hc, av=av, ldv=ldv, e_i=e_i):
                                e.wait_ge(s_ac.h, av)
                                if e_i < 16:
                                    e.wait_ge(s_ld.h, ldv)
                                e.tensor_tensor(sc32[:], fsl2[:], ps[1][:, 0:TB], ALU.mult)
                                if e_i < 16:
                                    s_vv.inc(e.tensor_tensor(hall[:, hc * TB:(hc + 1) * TB], sc32[:], cbb[:], ALU.mult), 1)
                                else:
                                    s_vv.inc(e.tensor_copy(hall[:, hc * TB:(hc + 1) * TB], sc32[:]), 1)

                with NB() as blk:
                    wt_done = {}
                    for e_i in range(18):
                        d_b = edw if e_i % 2 == 0 else edw2
                        wsm = s_wta if e_i % 2 == 0 else s_wtb
                        gd = (s_mm, wt_done[e_i - 2]) if e_i >= 2 else None
                        if e_i < 16:
                            wsm, wv = stream_w(blk, d_b[:], rearr(down_p[e_i]), guard=gd, sem=wsm)
                        else:
                            half = e_i - 16
                            wsm, wv = stream_w(blk, d_b[:], rearr(shdT[half * 512:(half + 1) * 512, :]), guard=gd, sem=wsm)

                        @blk.tensor
                        def _(e, e_i=e_i, wv=wv, d_b=d_b, wsm=wsm):
                            e.wait_ge(wsm.h, wv)
                            for kk in range(4):
                                hc = e_i * 4 + kk if e_i < 16 else 64 + (e_i - 16) * 4 + kk
                                for m in range(8):
                                    i = e.matmul(pall[m][:, 0:TB],
                                                 d_b[:, kk * 1024 + m * 128: kk * 1024 + (m + 1) * 128],
                                                 hall[:, hc * TB:(hc + 1) * TB],
                                                 start=(e_i == 0 and kk == 0),
                                                 stop=(e_i == 17 and kk == 3))
                            s_mm.inc(i, 1)
                        wt_done[e_i] = s_mm.v
                    mv = s_mm.v

                    for m in range(8):
                        xbuf = xs_a if m % 2 == 0 else xs_d
                        xsm = s_lda if m % 2 == 0 else s_ldb

                        @blk.sync
                        def _(e, m=m, c0=c0, xbuf=xbuf, xsm=xsm):
                            e.wait_ge(s_vv.h, max(0, s_vv.v - 1))
                            xsm.inc(e.dma_start(xbuf[:], x_hbm[m * 128:(m + 1) * 128, c0:c0 + TB]), 16)
                        ldv = (xsm, xsm.v)

                        scb = sc32 if m % 2 == 0 else sc32b

                        @blk.vector
                        def _(e, m=m, mv=mv, ldv=ldv, xbuf=xbuf, scb=scb):
                            e.wait_ge(s_mm.h, mv)
                            e.wait_ge(ldv[0].h, ldv[1])
                            if (m - 2, tt4) in mstd:
                                e.wait_ge(s_st.h, mstd[(m - 2, tt4)])
                            elif m < 2 and tt4 >= 1:
                                e.wait_ge(s_st.h, mstd[(6 + m, tt4 - 1)])
                            s_vv.inc(e.tensor_tensor(scb[:], pall[m][:, 0:TB], xbuf[:], ALU.add), 1)
                        vv = s_vv.v

                        @blk.sync
                        def _(e, m=m, c0=c0, vv=vv, scb=scb):
                            e.wait_ge(s_vv.h, vv)
                            s_st.inc(e.dma_start(yout[m * 128:(m + 1) * 128, c0:c0 + TB], scb[:]), 16)
                            mstd[(m, tt4)] = s_st.v

                    @blk.sync
                    def _(e):
                        s_st.w(e)

        with NB() as blk:
            @blk.sync
            def _(e):
                s_st.w(e)
    else:
        with NB() as blk:
            @blk.sync
            def _(e):
                src = x_hbm if upto >= 2 else xT
                for kk in range(NCH):
                    for tt4 in range(4):
                        c0 = tt4 * TB
                        s_ld.inc(e.dma_start(xs_a[:], src[kk * 128:(kk + 1) * 128, c0:c0 + TB]), 16)
                        s_ld.w(e)
                        s_st.inc(e.dma_start(yout[kk * 128:(kk + 1) * 128, c0:c0 + TB], xs_a[:]), 16)
                        s_st.w(e)

    return nc


def _bucket(d):
    if d < 16:
        return d
    v = 16 + int(np.float32(np.log(np.float32(d) / np.float32(16.0))
                            / np.float32(math.log(8.0)) * np.float32(16.0)))
    return min(v, 31)


def _host_consts():
    ident = np.eye(128, dtype=np.float32)
    pj = np.arange(128)[:, None] + 384 - np.arange(512)[None, :]
    mmask = np.where(pj < 0, np.float32(NEG), np.float32(0.0)).astype(np.float32)
    onehot = np.zeros((32, 128 * 512), dtype=np.float32)
    col = 0
    for p in range(128):
        for j in range(512):
            d = p + 384 - j
            if d >= 0:
                q = _bucket(d)
                onehot[q, col] += 8.0
                onehot[31, col] -= 8.0
            col += 1
    return ident, mmask, onehot


def kernel(**inp):
    f32 = np.float32
    c = np.ascontiguousarray
    ident, mmask, onehot = _host_consts()

    weights = {
        "tkT": c(np.asarray(inp["L0_attn_t_k"], f32).T),
        "tvT": c(np.asarray(inp["L0_attn_t_v"], f32).T),
        "tqT": c(np.asarray(inp["L0_attn_t_q"], f32).T),
        "w1T": c(np.asarray(inp["L0_attn_W1"], f32).T),
        "w2T": c(np.asarray(inp["L0_attn_W2"], f32).T),
        "b1h": c(np.asarray(inp["L0_attn_b1"], f32).reshape(2, 128).T),
        "b2h": c(np.asarray(inp["L0_attn_b2"], f32).reshape(8, 128).T),
        "wgT": c(np.asarray(inp["L0_ffn_w_gate"]).T.astype(np.int32)),
        "wuT": c(np.asarray(inp["L0_ffn_w_up"]).T.astype(np.int32)),
        "wdT": c(np.asarray(inp["L0_ffn_w_down"]).T.astype(np.int32)),
        "wattn": c(np.asarray(inp["L1_attn_W_attn"], f32)),
        "battn": c(np.asarray(inp["L1_attn_b_attn"], f32).reshape(24, 128).T),
        "wproj": c(np.asarray(inp["L1_attn_W_proj"], f32)),
        "bproj": c(np.asarray(inp["L1_attn_b_proj"], f32).reshape(8, 128).T),
        "btab": c(np.asarray(inp["L1_attn_bias_table"], f32)),
        "rwT": c(np.asarray(inp["L1_ffn_router_w"], f32).T),
        "rbias": c(np.asarray(inp["L1_ffn_router_bias"], f32).reshape(16, 1)),
        "up_p": c(np.asarray(inp["L1_ffn_up_proj"], f32)),
        "gate_p": c(np.asarray(inp["L1_ffn_gate_proj"], f32)),
        "down_p": c(np.asarray(inp["L1_ffn_down_proj"], f32)),
        "shgT": c(np.asarray(inp["L1_ffn_sh_gate"], f32).T),
        "shuT": c(np.asarray(inp["L1_ffn_sh_up"], f32).T),
        "shdT": c(np.asarray(inp["L1_ffn_sh_down"], f32).T),
        "ident_in": ident,
        "mmask": mmask,
        "onehotT": onehot,
    }

    x = np.asarray(inp["L0_attn_in_seq"], f32)
    in_maps = []
    for core in range(NCORES):
        m = dict(weights)
        m["xT"] = c(x[:, core * TB:(core + 1) * TB, :].reshape(TC, D).T)
        in_maps.append(m)

    nc = build_nc(upto=PH_ALL)
    res = run_bass_kernel_spmd(nc, in_maps, core_ids=list(range(NCORES)))

    out = np.empty((B, TFULL, D), dtype=f32)
    for core in range(NCORES):
        yc = res.results[core]["yout"]
        out[:, core * TB:(core + 1) * TB, :] = yc.T.reshape(B, TB, D)
    return out


# version 04 (final)
# speedup vs baseline: 1.0024x; 1.0024x over previous
import math
from contextlib import ExitStack
import numpy as np
import concourse.bass as bass
import concourse.mybir as mybir
from concourse.bass_utils import run_bass_kernel_spmd

F32 = mybir.dt.float32
BF16 = mybir.dt.bfloat16
I32 = mybir.dt.int32
ALU = mybir.AluOpType
AF = mybir.ActivationFunctionType
AX = mybir.AxisListType

NCORES = 8
B = 4
TFULL = 4096
D = 1024
TC = 2048
TB = 512
NCH = 8
LN_EPS = 1e-5
LRC = 0.001 * 2.0 / (4096.0 * 1024.0)
QS = 1.0 / 127.0
NEG = -1.0e30
RG = [list(range(NCORES))]
PH_ALL = 99
DUMP = False


class Sem:
    def __init__(self, nc, name):
        self.h = nc.alloc_semaphore(name)
        self.v = 0

    def inc(self, inst, n=1):
        inst.then_inc(self.h, n)
        self.v += n
        return self.v

    def w(self, eng, val=None):
        eng.wait_ge(self.h, self.v if val is None else val)


def build_nc(upto=PH_ALL):
    nc = bass.Bass("TRN2", target_bir_lowering=False)

    xT = nc.dram_tensor("xT", [D, TC], F32, kind="ExternalInput")
    tkT = nc.dram_tensor("tkT", [D, D], F32, kind="ExternalInput")
    tvT = nc.dram_tensor("tvT", [D, D], F32, kind="ExternalInput")
    tqT = nc.dram_tensor("tqT", [D, D], F32, kind="ExternalInput")
    w1T = nc.dram_tensor("w1T", [D, 256], F32, kind="ExternalInput")
    w2T = nc.dram_tensor("w2T", [256, D], F32, kind="ExternalInput")
    b1h = nc.dram_tensor("b1h", [128, 2], F32, kind="ExternalInput")
    b2h = nc.dram_tensor("b2h", [128, 8], F32, kind="ExternalInput")
    wgT = nc.dram_tensor("wgT", [D, 3072], I32, kind="ExternalInput")
    wuT = nc.dram_tensor("wuT", [D, 3072], I32, kind="ExternalInput")
    wdT = nc.dram_tensor("wdT", [3072, D], I32, kind="ExternalInput")
    wattn = nc.dram_tensor("wattn", [D, 3 * D], F32, kind="ExternalInput")
    battn = nc.dram_tensor("battn", [128, 24], F32, kind="ExternalInput")
    wproj = nc.dram_tensor("wproj", [D, D], F32, kind="ExternalInput")
    bproj = nc.dram_tensor("bproj", [128, 8], F32, kind="ExternalInput")
    btab = nc.dram_tensor("btab", [32, 16], F32, kind="ExternalInput")
    rwT = nc.dram_tensor("rwT", [D, 16], F32, kind="ExternalInput")
    rbias = nc.dram_tensor("rbias", [16, 1], F32, kind="ExternalInput")
    up_p = nc.dram_tensor("up_p", [16, D, 512], F32, kind="ExternalInput")
    gate_p = nc.dram_tensor("gate_p", [16, D, 512], F32, kind="ExternalInput")
    down_p = nc.dram_tensor("down_p", [16, 512, D], F32, kind="ExternalInput")
    shgT = nc.dram_tensor("shgT", [D, 1024], F32, kind="ExternalInput")
    shuT = nc.dram_tensor("shuT", [D, 1024], F32, kind="ExternalInput")
    shdT = nc.dram_tensor("shdT", [1024, D], F32, kind="ExternalInput")
    ident_in = nc.dram_tensor("ident_in", [128, 128], F32, kind="ExternalInput")
    mmask = nc.dram_tensor("mmask", [128, 512], F32, kind="ExternalInput")
    onehotT = nc.dram_tensor("onehotT", [32, 128 * 512], F32, kind="ExternalInput")

    yout = nc.dram_tensor("yout", [D, TC], F32, kind="ExternalOutput")
    dbg1 = nc.dram_tensor("dbg1", [128, 4096], F32, kind="ExternalOutput")
    dbg2 = nc.dram_tensor("dbg2", [128, 4096], F32, kind="ExternalOutput")
    dbg3 = nc.dram_tensor("dbg3", [128, 2048], F32, kind="ExternalOutput")

    x_hbm = nc.dram_tensor("x_hbm", [D, TC], F32, kind="Internal")
    a_hbm = nc.dram_tensor("a_hbm", [16 * 128, 512], F32, kind="Internal")
    rows_hbm = nc.dram_tensor("rows_hbm", [2, TB], F32, kind="Internal")
    sc_hbm = nc.dram_tensor("sc_hbm", [2, 2], F32, kind="Internal")
    cmb_hbm = nc.dram_tensor("cmb_hbm", [16, TC], F32, kind="Internal")
    NAR = 1024 * 256 + 256 * 1024 + 1280
    ar_in = nc.dram_tensor("ar_in", [NAR], F32, kind="Internal")
    ar_out = nc.dram_tensor("ar_out", [NAR], F32, kind="Internal")
    ag1_in = nc.dram_tensor("ag1_in", [3 * D, TC // 2], I32, kind="Internal")
    ag1_out = nc.dram_tensor("ag1_out", [NCORES * 3 * D, TC // 2], I32, kind="Internal")
    ag2_in = nc.dram_tensor("ag2_in", [128, B * TFULL // 2], I32, kind="Internal")
    ag2_out = nc.dram_tensor("ag2_out", [NCORES * 128, B * TFULL // 2], I32, kind="Internal")
    ag3_in = nc.dram_tensor("ag3_in", [2, B * TFULL], F32, kind="Internal")
    ag3_out = nc.dram_tensor("ag3_out", [16, B * TFULL], F32, kind="Internal")
    myqkv = nc.dram_tensor("myqkv", [384, 8192], I32, kind="Internal")
    myy = nc.dram_tensor("myy", [1024, 1024], I32, kind="Internal")
    myl = nc.dram_tensor("myl", [16, TC], F32, kind="Internal")
    br_in = nc.dram_tensor("br_in", [128], F32, kind="Internal")
    br_out = nc.dram_tensor("br_out", [128], F32, kind="Internal")

    xn = nc.alloc_sbuf_tensor("xn", [128, NCH * TC], BF16)
    sc32 = nc.alloc_sbuf_tensor("sc32", [128, 512], F32)
    sc32b = nc.alloc_sbuf_tensor("sc32b", [128, 512], F32)
    mb = nc.alloc_sbuf_tensor("mb", [128, 512], F32)
    ib = nc.alloc_sbuf_tensor("ib", [128, 512], F32)
    r_sum = nc.alloc_sbuf_tensor("r_sum", [1, TB], F32)
    r_sq = nc.alloc_sbuf_tensor("r_sq", [1, TB], F32)
    r_mean = nc.alloc_sbuf_tensor("r_mean", [1, TB], F32)
    r_inv = nc.alloc_sbuf_tensor("r_inv", [1, TB], F32)
    xs_a = nc.alloc_sbuf_tensor("xs_a", [128, 512], F32)
    xs_d = nc.alloc_sbuf_tensor("xs_d", [128, 512], F32)
    xs_b = nc.alloc_sbuf_tensor("xs_b", [128, 512], BF16)
    xs_c = nc.alloc_sbuf_tensor("xs_c", [128, 512], BF16)
    ident = nc.alloc_sbuf_tensor("ident", [128, 128], BF16)
    identf = nc.alloc_sbuf_tensor("identf", [128, 128], F32)
    onescol = nc.alloc_sbuf_tensor("onescol", [128, 1], BF16)
    b1c = nc.alloc_sbuf_tensor("b1c", [128, 2], F32)
    b2c = nc.alloc_sbuf_tensor("b2c", [128, 8], F32)
    battnc = nc.alloc_sbuf_tensor("battnc", [128, 24], F32)
    bprojc = nc.alloc_sbuf_tensor("bprojc", [128, 8], F32)
    rbc = nc.alloc_sbuf_tensor("rbc", [16, 1], F32)
    epsc = nc.alloc_sbuf_tensor("epsc", [1, 1], F32)

    ps = [nc.alloc_psum_tensor(f"ps{i}", [128, 512], F32) for i in range(6)]

    s_ld = Sem(nc, "s_ld")
    s_wt = Sem(nc, "s_wt")
    s_mm = Sem(nc, "s_mm")
    s_ac = Sem(nc, "s_ac")
    s_vv = Sem(nc, "s_vv")
    s_st = Sem(nc, "s_st")
    s_cc = Sem(nc, "s_cc")
    s_cb = Sem(nc, "s_cb")
    s_lda = Sem(nc, "s_lda")
    s_ldb = Sem(nc, "s_ldb")
    s_bc = Sem(nc, "s_bc")
    s_sg = Sem(nc, "s_sg")
    s_wta = Sem(nc, "s_wta")
    s_wtb = Sem(nc, "s_wtb")

    from contextlib import contextmanager

    @contextmanager
    def NB():
        with nc.Block() as blk:
            vv, av = s_vv.v, s_ac.v

            def w2(e, vv=vv, av=av):
                e.wait_ge(s_vv.h, vv)
                e.wait_ge(s_ac.h, av)
            blk.tensor(w2)
            blk.sync(w2)
            blk.scalar(lambda e, vv=vv: e.wait_ge(s_vv.h, vv))
            blk.vector(lambda e, av=av: e.wait_ge(s_ac.h, av))
            blk.gpsimd(w2)
            yield blk

    def ln_pass(src_dram):
        for tt4 in range(4):
            c0 = tt4 * TB
            with NB() as blk:
                stat_mm = [0] * NCH
                cast_done = [0] * NCH
                for kk in range(NCH):
                    xbuf = xs_a if kk % 2 == 0 else xs_d
                    xsm = s_lda if kk % 2 == 0 else s_ldb

                    @blk.sync
                    def _(e, kk=kk, xbuf=xbuf, xsm=xsm):
                        if kk >= 2:
                            e.wait_ge(s_vv.h, cast_done[kk - 2])
                        xsm.inc(e.dma_start(xbuf[:], src_dram[kk * 128:(kk + 1) * 128, c0:c0 + TB]), 16)
                    ldv = xsm.v

                    @blk.vector
                    def _(e, kk=kk, ldv=ldv, xbuf=xbuf, xsm=xsm):
                        xsm.w(e, ldv)
                        if kk >= 1:
                            e.wait_ge(s_mm.h, stat_mm[kk - 1])
                        e.tensor_copy(xs_b[:], xbuf[:])
                        s_vv.inc(e.tensor_tensor(xs_c[:], xs_b[:], xs_b[:], ALU.mult), 1)
                        cast_done[kk] = s_vv.v
                    vv = s_vv.v

                    @blk.tensor
                    def _(e, kk=kk, vv=vv):
                        e.wait_ge(s_vv.h, vv)
                        e.matmul(ps[4][0:1, :], onescol[:], xs_b[:], start=(kk == 0), stop=(kk == NCH - 1))
                        s_mm.inc(e.matmul(ps[5][0:1, :], onescol[:], xs_c[:], start=(kk == 0), stop=(kk == NCH - 1)), 1)
                        stat_mm[kk] = s_mm.v
                mmv = s_mm.v

                @blk.vector
                def _(e, mmv=mmv):
                    e.wait_ge(s_mm.h, mmv)
                    e.tensor_scalar(r_mean[:], ps[4][0:1, :], 1.0 / D, None, ALU.mult)
                    s_vv.inc(e.tensor_scalar(r_sq[:], ps[5][0:1, :], 1.0 / D, None, ALU.mult), 1)
                    s_vv.w(e)
                    e.tensor_tensor(r_sum[:], r_mean[:], r_mean[:], ALU.mult)
                    s_vv.inc(e.tensor_tensor(r_sq[:], r_sq[:], r_sum[:], ALU.subtract), 1)
                vv = s_vv.v

                @blk.scalar
                def _(e, vv=vv):
                    e.wait_ge(s_vv.h, vv)
                    s_ac.inc(e.activation(r_sq[:], r_sq[:], AF.Sqrt, bias=epsc[:], scale=1.0), 1)
                acv = s_ac.v

                @blk.vector
                def _(e, acv=acv):
                    e.wait_ge(s_ac.h, acv)
                    s_vv.inc(e.reciprocal(r_inv[:], r_sq[:]), 1)
                vv = s_vv.v

                @blk.sync
                def _(e, vv=vv):
                    e.wait_ge(s_vv.h, vv)
                    s_bc.inc(e.dma_start(rows_hbm[0:1, :], r_mean[:]), 16)
                    s_bc.inc(e.dma_start(rows_hbm[1:2, :], r_inv[:]), 16)
                    s_bc.w(e)
                    s_bc.inc(e.dma_start(mb[:], rows_hbm[0:1, :].broadcast_to((128, TB))), 16)
                    s_bc.inc(e.dma_start(ib[:], rows_hbm[1:2, :].broadcast_to((128, TB))), 16)
                bcv = s_bc.v

                apply_done = [0] * NCH
                for kk in range(NCH):
                    xbuf = xs_a if kk % 2 == 0 else xs_d
                    xsm = s_lda if kk % 2 == 0 else s_ldb

                    @blk.sync
                    def _(e, kk=kk, xbuf=xbuf, xsm=xsm):
                        if kk >= 2:
                            e.wait_ge(s_vv.h, apply_done[kk - 2])
                        xsm.inc(e.dma_start(xbuf[:], src_dram[kk * 128:(kk + 1) * 128, c0:c0 + TB]), 16)
                    ldv2 = xsm.v

                    @blk.vector
                    def _(e, kk=kk, ldv2=ldv2, xbuf=xbuf, xsm=xsm, bcv=bcv):
                        xsm.w(e, ldv2)
                        e.wait_ge(s_bc.h, bcv)
                        e.tensor_tensor(sc32[:], xbuf[:], mb[:], ALU.subtract)
                        s_vv.inc(e.tensor_tensor(xn[:, kk * TC + c0: kk * TC + c0 + TB], sc32[:], ib[:], ALU.mult), 1)
                        apply_done[kk] = s_vv.v

    def stream_w(blk, dst_ap, src_ap, guard=None, sem=None):
        sem = sem if sem is not None else s_wt
        mwid = src_ap.shape[1]
        src3 = src_ap.rearrange("(kk p) m -> p kk m", p=128)
        dst3 = dst_ap.rearrange("p (kk m) -> p kk m", m=mwid)

        @blk.gpsimd
        def _(e):
            if guard is not None:
                e.wait_ge(guard[0].h, guard[1])
            sem.inc(e.dma_start(dst3, src3), 16)
        return (sem, sem.v)

    def rearr(t):
        return t

    with NB() as blk:
        @blk.sync
        def _(e):
            s_ld.inc(e.dma_start(identf[:], ident_in[:]), 16)
            s_ld.inc(e.dma_start(b1c[:], b1h[:]), 16)
            s_ld.inc(e.dma_start(b2c[:], b2h[:]), 16)
            s_ld.inc(e.dma_start(battnc[:], battn[:]), 16)
            s_ld.inc(e.dma_start(bprojc[:], bproj[:]), 16)
            s_ld.inc(e.dma_start(rbc[:], rbias[:]), 16)

        @blk.gpsimd
        def _(e):
            s_wt.inc(e.dma_start(ident[:], ident_in[:]), 16)

        @blk.vector
        def _(e):
            e.memset(onescol[:], 1.0)
            e.memset(epsc[:], LN_EPS)

        @blk.sync
        def _(e):
            s_ld.w(e)

        @blk.gpsimd
        def _(e):
            s_wt.w(e)

    if upto >= 1:
        with nc.sbuf_tensor("btab_sb", [32, 16], BF16) as btab_sb, \
             nc.sbuf_tensor("oh_sb", [32, 4 * 512], BF16) as oh_sb, \
             nc.sbuf_tensor("ab_ev", [16, 512], F32) as ab_ev, \
             nc.Block() as blk:
            @blk.gpsimd
            def _(e):
                s_wt.inc(e.dma_start(btab_sb[:], btab[:]), 16)
            wtv0 = s_wt.v
            mm_done = [0] * 128
            ev_done = [0] * 128
            st_done = [0] * 128
            for p in range(128):
                buf = p % 4

                @blk.gpsimd
                def _(e, p=p, buf=buf):
                    if p >= 4:
                        e.wait_ge(s_mm.h, mm_done[p - 4])
                    s_wt.inc(e.dma_start(oh_sb[:, buf * 512:(buf + 1) * 512],
                                         onehotT[:, p * 512:(p + 1) * 512]), 16)
                wv = s_wt.v

                @blk.tensor
                def _(e, p=p, buf=buf, wv=wv):
                    e.wait_ge(s_wt.h, max(wv, wtv0))
                    if p >= 1:
                        e.wait_ge(s_ac.h, ev_done[p - 1])
                    s_mm.inc(e.matmul(ps[4][0:16, :], btab_sb[:], oh_sb[:, buf * 512:(buf + 1) * 512],
                                      start=True, stop=True), 1)
                    mm_done[p] = s_mm.v
                mv = s_mm.v

                @blk.scalar
                def _(e, p=p, mv=mv):
                    e.wait_ge(s_mm.h, mv)
                    if p >= 1:
                        e.wait_ge(s_ld.h, st_done[p - 1])
                    s_ac.inc(e.copy(ab_ev[:], ps[4][0:16, :]), 1)
                    ev_done[p] = s_ac.v
                av = s_ac.v

                @blk.sync
                def _(e, p=p, av=av):
                    e.wait_ge(s_ac.h, av)
                    s_ld.inc(e.dma_start(a_hbm[:, :].rearrange("(h p) j -> h (p j)", p=128)[:, p * 512:(p + 1) * 512],
                                         ab_ev[:]), 16)
                    st_done[p] = s_ld.v

            @blk.sync
            def _(e):
                s_ld.w(e)

        ln_pass(xT)

    if upto >= 2:
        with ExitStack() as _st:
            _e = _st.enter_context
            wstr = _e(nc.sbuf_tensor("wstr", [128, 1024], BF16))
            wstr2 = _e(nc.sbuf_tensor("wstr2", [128, 1024], BF16))
            w1t32 = _e(nc.sbuf_tensor("w1t32", [128, 2048], F32))
            w2t32 = _e(nc.sbuf_tensor("w2t32", [128, 2048], F32))
            w1tb = _e(nc.sbuf_tensor("w1tb", [128, 2048], BF16))
            w2tb = _e(nc.sbuf_tensor("w2tb", [128, 2048], BF16))
            w2b = _e(nc.sbuf_tensor("w2b", [128, 2048], BF16))
            zf = _e(nc.sbuf_tensor("zf", [128, 4096], F32))
            qf = _e(nc.sbuf_tensor("qf", [128, 4096], F32))
            zb = _e(nc.sbuf_tensor("zb", [128, 4096], BF16))
            qb = _e(nc.sbuf_tensor("qb", [128, 4096], BF16))
            grb = _e(nc.sbuf_tensor("grb", [128, 4096], BF16))
            af = _e(nc.sbuf_tensor("af", [128, 1024], F32))
            hb = _e(nc.sbuf_tensor("hb", [128, 1024], BF16))
            gp1 = _e(nc.sbuf_tensor("gp1", [128, 1024], F32))
            gp2 = _e(nc.sbuf_tensor("gp2", [128, 1024], F32))
            gab = _e(nc.sbuf_tensor("gab", [128, 1024], BF16))
            gsgn = _e(nc.sbuf_tensor("gsgn", [128, 1024], F32))
            ztm = _e(nc.sbuf_tensor("ztm", [128, 4096], BF16))
            grtm = _e(nc.sbuf_tensor("grtm", [128, 4096], BF16))
            htm = _e(nc.sbuf_tensor("htm", [128, 1024], BF16))
            gatm = _e(nc.sbuf_tensor("gatm", [128, 1024], BF16))
            dw1 = _e(nc.sbuf_tensor("dw1", [128, 2048], F32))
            dw2 = _e(nc.sbuf_tensor("dw2", [128, 2048], F32))
            dbs = _e(nc.sbuf_tensor("dbs", [128, 10], F32))
            psTb = _e(nc.psum_tensor("psTb", [128, 512], BF16))

            with NB() as blk:
                @blk.sync
                def _(e):
                    for kk in range(8):
                        s_ld.inc(e.dma_start(w1t32[:, kk * 256:(kk + 1) * 256],
                                             w1T[kk * 128:(kk + 1) * 128, :]), 16)
                    for kk in range(2):
                        s_ld.inc(e.dma_start(w2t32[:, kk * 1024:(kk + 1) * 1024],
                                             w2T[kk * 128:(kk + 1) * 128, :]), 16)
                ldv = s_ld.v

                @blk.vector
                def _(e, ldv=ldv):
                    s_ld.w(e, ldv)
                    for kk in range(8):
                        e.tensor_copy(w1tb[:, kk * 256:(kk + 1) * 256], w1t32[:, kk * 256:(kk + 1) * 256])
                    for kk in range(2):
                        s_vv.inc(e.tensor_copy(w2tb[:, kk * 1024:(kk + 1) * 1024],
                                               w2t32[:, kk * 1024:(kk + 1) * 1024]), 1)

            for step in range(B):
                c0 = step * TB

                with NB() as blk:
                    for kk in range(2):
                        for m in range(8):
                            @blk.tensor
                            def _(e, kk=kk, m=m):
                                e.wait_ge(s_vv.h, s_vv.v)
                                s_mm.inc(e.transpose(psTb[:, 0:128],
                                                     w2tb[:, kk * 1024 + m * 128: kk * 1024 + (m + 1) * 128],
                                                     ident[:]), 1)
                            mv = s_mm.v

                            @blk.vector
                            def _(e, kk=kk, m=m, mv=mv):
                                e.wait_ge(s_mm.h, mv)
                                s_vv.inc(e.tensor_copy(w2b[:, m * 256 + kk * 128: m * 256 + (kk + 1) * 128],
                                                       psTb[:, 0:128]), 1)

                with NB() as blk:
                    ev_done = {}
                    for wi, (wsrc, outf, outb) in enumerate(((tkT, zf, zb), (tqT, qf, qb))):
                        for m in range(8):
                            idx = wi * 8 + m
                            wb = wstr if idx % 2 == 0 else wstr2
                            wsm = s_wta if idx % 2 == 0 else s_wtb
                            g = (s_mm, ev_done[idx - 2][0]) if idx >= 2 else None
                            wsm, wv = stream_w(blk, wb[:], rearr(wsrc[:, m * 128:(m + 1) * 128]), guard=g, sem=wsm)

                            @blk.tensor
                            def _(e, m=m, idx=idx, wv=wv, wb=wb, wsm=wsm):
                                e.wait_ge(wsm.h, wv)
                                if idx >= 2:
                                    e.wait_ge(s_ac.h, ev_done[idx - 2][1])
                                pst = ps[idx % 2]
                                for kk in range(8):
                                    i = e.matmul(pst[:, 0:TB], wb[:, kk * 128:(kk + 1) * 128],
                                                 xn[:, kk * TC + c0: kk * TC + c0 + TB],
                                                 start=(kk == 0), stop=(kk == 7))
                                s_mm.inc(i, 1)
                            mv = s_mm.v

                            @blk.scalar
                            def _(e, m=m, idx=idx, mv=mv, outf=outf):
                                e.wait_ge(s_mm.h, mv)
                                s_ac.inc(e.copy(outf[:, m * 512:(m + 1) * 512], ps[idx % 2][:, 0:TB]), 1)
                            av = s_ac.v
                            ev_done[idx] = (mv, av)

                            @blk.vector
                            def _(e, m=m, av=av, outf=outf, outb=outb):
                                e.wait_ge(s_ac.h, av)
                                s_vv.inc(e.tensor_copy(outb[:, m * 512:(m + 1) * 512],
                                                       outf[:, m * 512:(m + 1) * 512]), 1)

                if DUMP and step == 0:
                    with NB() as blk:
                        @blk.sync
                        def _(e):
                            s_st.inc(e.dma_start(dbg1[:, :], qf[:]), 16)
                            s_st.inc(e.dma_start(dbg2[:, :], zf[:]), 16)
                            for kk in range(8):
                                s_st.inc(e.dma_start(
                                    dbg3[:, kk * 256:(kk + 1) * 256].bitcast(I32),
                                    xn[:, kk * TC: kk * TC + 512].bitcast(I32)), 16)
                            s_st.w(e)

                with NB() as blk:
                    for m in range(2):
                        @blk.tensor
                        def _(e, m=m):
                            for kk in range(8):
                                i = e.matmul(ps[m][:, 0:TB], w1tb[:, kk * 256 + m * 128: kk * 256 + (m + 1) * 128],
                                             zb[:, kk * 512:(kk + 1) * 512], start=(kk == 0), stop=(kk == 7))
                            s_mm.inc(i, 1)
                        mv = s_mm.v

                        @blk.scalar
                        def _(e, m=m, mv=mv):
                            e.wait_ge(s_mm.h, mv)
                            e.activation(af[:, m * 512:(m + 1) * 512], ps[m][:, 0:TB], AF.Identity,
                                         bias=b1c[:, m:m + 1], scale=1.0)
                            s_ac.inc(e.activation(hb[:, m * 512:(m + 1) * 512], ps[m][:, 0:TB], AF.Gelu,
                                                  bias=b1c[:, m:m + 1], scale=1.0), 1)

                with NB() as blk:
                    done = {}
                    for m in range(8):
                        wb = wstr if m % 2 == 0 else wstr2
                        wsm = s_wta if m % 2 == 0 else s_wtb
                        g = (s_vv, done[m - 2]) if m >= 2 else None
                        wsm, wv = stream_w(blk, wb[:], rearr(tvT[:, m * 128:(m + 1) * 128]), guard=g, sem=wsm)
                        pr = ps[2] if m % 2 == 0 else ps[4]
                        py = ps[3] if m % 2 == 0 else ps[5]

                        @blk.tensor
                        def _(e, m=m, wv=wv, wb=wb, pr=pr, py=py, wsm=wsm):
                            e.wait_ge(wsm.h, wv)
                            if m >= 2:
                                e.wait_ge(s_vv.h, done[m - 2])
                            for kk in range(2):
                                e.matmul(pr[:, 0:TB], w2tb[:, kk * 1024 + m * 128: kk * 1024 + (m + 1) * 128],
                                         hb[:, kk * 512:(kk + 1) * 512], start=(kk == 0), stop=(kk == 1))
                            for kk in range(8):
                                i = e.matmul(py[:, 0:TB], wb[:, kk * 128:(kk + 1) * 128],
                                             xn[:, kk * TC + c0: kk * TC + c0 + TB],
                                             start=(kk == 0), stop=(kk == 7))
                            s_mm.inc(i, 1)
                        mv = s_mm.v

                        @blk.scalar
                        def _(e, m=m, mv=mv, pr=pr):
                            e.wait_ge(s_mm.h, mv)
                            s_ac.inc(e.activation(sc32[:] if m % 2 == 0 else sc32b[:], pr[:, 0:TB],
                                                  AF.Identity, bias=b2c[:, m:m + 1], scale=1.0), 1)
                        av = s_ac.v

                        @blk.vector
                        def _(e, m=m, av=av, py=py):
                            e.wait_ge(s_ac.h, av)
                            sc = sc32 if m % 2 == 0 else sc32b
                            e.tensor_tensor(sc[:], sc[:], zf[:, m * 512:(m + 1) * 512], ALU.add)
                            s_vv.inc(e.tensor_tensor(grb[:, m * 512:(m + 1) * 512], sc[:], py[:, 0:TB], ALU.subtract), 1)
                            done[m] = s_vv.v

                with NB() as blk:
                    @blk.scalar
                    def _(e):
                        e.activation(gp1[:], af[:], AF.Square)
                        s_ac.inc(e.activation(gp1[:], gp1[:], AF.Exp, bias=0.0, scale=-0.5), 1)
                    av = s_ac.v

                    @blk.scalar
                    def _(e):
                        e.activation(gsgn[:], af[:], AF.Sign)
                        s_ac.inc(e.activation(gp2[:], af[:], AF.Abs), 1)
                    sgv = s_ac.v

                    @blk.vector
                    def _(e, av=av, sgv=sgv):
                        e.wait_ge(s_ac.h, max(av, sgv))
                        e.tensor_scalar(gp2[:], gp2[:], 1e-9, None, ALU.max)
                        s_vv.inc(e.reciprocal(gp2[:], gp2[:]), 1)
                        s_vv.w(e)
                        e.wait_ge(s_ac.h, sgv)
                        e.tensor_tensor(gp2[:], gp2[:], gsgn[:], ALU.mult)
                        e.tensor_tensor(gp2[:], hb[:], gp2[:], ALU.mult)
                        e.scalar_tensor_tensor(gp1[:], af[:], 0.3989422804014327, gp1[:], ALU.mult, ALU.mult)
                        s_vv.inc(e.tensor_tensor(gp2[:], gp2[:], gp1[:], ALU.add), 1)
                    vv = s_vv.v

                    for m in range(2):
                        @blk.tensor
                        def _(e, m=m, vv=vv):
                            e.wait_ge(s_vv.h, vv)
                            for kk in range(8):
                                i = e.matmul(ps[m][:, 0:TB], w2b[:, kk * 256 + m * 128: kk * 256 + (m + 1) * 128],
                                             grb[:, kk * 512:(kk + 1) * 512], start=(kk == 0), stop=(kk == 7))
                            s_mm.inc(i, 1)
                        mv = s_mm.v

                        @blk.vector
                        def _(e, m=m, mv=mv):
                            e.wait_ge(s_mm.h, mv)
                            s_vv.inc(e.tensor_tensor(gab[:, m * 512:(m + 1) * 512], ps[m][:, 0:TB],
                                                     gp2[:, m * 512:(m + 1) * 512], ALU.mult), 1)

                with NB() as blk:
                    for (src, dst, nkk) in ((zb, ztm, 8), (grb, grtm, 8), (hb, htm, 2), (gab, gatm, 2)):
                        for kk in range(nkk):
                            for t4 in range(4):
                                @blk.tensor
                                def _(e, src=src, kk=kk, t4=t4):
                                    e.wait_ge(s_vv.h, s_vv.v)
                                    s_mm.inc(e.transpose(psTb[:, 0:128],
                                                         src[:, kk * 512 + t4 * 128: kk * 512 + (t4 + 1) * 128],
                                                         ident[:]), 1)
                                mv = s_mm.v

                                @blk.vector
                                def _(e, dst=dst, nkk=nkk, kk=kk, t4=t4, mv=mv):
                                    e.wait_ge(s_mm.h, mv)
                                    w = nkk * 128
                                    s_vv.inc(e.tensor_copy(dst[:, t4 * w + kk * 128: t4 * w + (kk + 1) * 128],
                                                           psTb[:, 0:128]), 1)

                with NB() as blk:
                    for m in range(8):
                        @blk.tensor
                        def _(e, m=m):
                            if m >= 2:
                                e.wait_ge(s_vv.h, s_vv.v)
                            for t4 in range(4):
                                i = e.matmul(ps[m % 2][:, 0:256],
                                             ztm[:, t4 * 1024 + m * 128: t4 * 1024 + (m + 1) * 128],
                                             gatm[:, t4 * 256:(t4 + 1) * 256],
                                             start=(t4 == 0), stop=(t4 == 3))
                            s_mm.inc(i, 1)
                        mv = s_mm.v

                        @blk.vector
                        def _(e, m=m, mv=mv):
                            e.wait_ge(s_mm.h, mv)
                            s_vv.inc(e.tensor_copy(dw1[:, m * 256:(m + 1) * 256], ps[m % 2][:, 0:256]), 1)
                    for m in range(2):
                        for dh_ in range(2):
                            @blk.tensor
                            def _(e, m=m, dh_=dh_):
                                e.wait_ge(s_vv.h, s_vv.v)
                                for t4 in range(4):
                                    i = e.matmul(ps[2 + (2 * m + dh_) % 2][:, 0:512],
                                                 htm[:, t4 * 256 + m * 128: t4 * 256 + (m + 1) * 128],
                                                 grtm[:, t4 * 1024 + dh_ * 512: t4 * 1024 + (dh_ + 1) * 512],
                                                 start=(t4 == 0), stop=(t4 == 3))
                                s_mm.inc(i, 1)
                            mv = s_mm.v

                            @blk.vector
                            def _(e, m=m, dh_=dh_, mv=mv):
                                e.wait_ge(s_mm.h, mv)
                                s_vv.inc(e.tensor_copy(dw2[:, m * 1024 + dh_ * 512: m * 1024 + (dh_ + 1) * 512],
                                                       ps[2 + (2 * m + dh_) % 2][:, 0:512]), 1)

                    @blk.vector
                    def _(e):
                        for kk in range(8):
                            s_vv.inc(e.reduce_sum(dbs[:, kk:kk + 1], grb[:, kk * 512:(kk + 1) * 512], axis=AX.X), 1)
                        for m in range(2):
                            s_vv.inc(e.reduce_sum(dbs[:, 8 + m:9 + m], gab[:, m * 512:(m + 1) * 512], axis=AX.X), 1)

                with NB() as blk:
                    @blk.sync
                    def _(e):
                        e.wait_ge(s_vv.h, s_vv.v)
                        ariv = ar_in[:].rearrange("(p n) -> p n", p=128)
                        s_ld.inc(e.dma_start(ariv[:, 0:2048], dw1[:]), 16)
                        s_ld.inc(e.dma_start(ariv[:, 2048:4096], dw2[:]), 16)
                        s_ld.inc(e.dma_start(ariv[:, 4096:4106], dbs[:]), 16)
                    ldv = s_ld.v

                    @blk.gpsimd
                    def _(e, ldv=ldv):
                        e.wait_ge(s_ld.h, ldv)
                        s_cc.inc(e.collective_compute("AllReduce", ALU.add, RG, [ar_in[:]], [ar_out[:]]), 1)
                    ccv = s_cc.v

                    @blk.sync
                    def _(e, ccv=ccv):
                        e.wait_ge(s_cc.h, ccv)
                        arv = ar_out[:].rearrange("(p n) -> p n", p=128)
                        s_ld.inc(e.dma_start(dw1[:], arv[:, 0:2048]), 16)
                        s_ld.inc(e.dma_start(dw2[:], arv[:, 2048:4096]), 16)
                        s_ld.inc(e.dma_start(dbs[:], arv[:, 4096:4106]), 16)
                    ldv = s_ld.v

                    @blk.vector
                    def _(e, ldv=ldv):
                        e.wait_ge(s_ld.h, ldv)
                        e.scalar_tensor_tensor(w1t32[:], dw1[:], -LRC, w1t32[:], ALU.mult, ALU.add)
                        e.scalar_tensor_tensor(w2t32[:], dw2[:], -LRC, w2t32[:], ALU.mult, ALU.add)
                        e.scalar_tensor_tensor(b2c[:], dbs[:, 0:8], -LRC, b2c[:], ALU.mult, ALU.add)
                        e.scalar_tensor_tensor(b1c[:], dbs[:, 8:10], -LRC, b1c[:], ALU.mult, ALU.add)
                        for kk in range(8):
                            e.tensor_copy(w1tb[:, kk * 256:(kk + 1) * 256], w1t32[:, kk * 256:(kk + 1) * 256])
                        for kk in range(2):
                            s_vv.inc(e.tensor_copy(w2tb[:, kk * 1024:(kk + 1) * 1024],
                                                   w2t32[:, kk * 1024:(kk + 1) * 1024]), 1)

                with NB() as blk:
                    for m in range(2):
                        @blk.tensor
                        def _(e, m=m):
                            e.wait_ge(s_vv.h, s_vv.v)
                            for kk in range(8):
                                i = e.matmul(ps[m][:, 0:TB], w1tb[:, kk * 256 + m * 128: kk * 256 + (m + 1) * 128],
                                             qb[:, kk * 512:(kk + 1) * 512], start=(kk == 0), stop=(kk == 7))
                            s_mm.inc(i, 1)
                        mv = s_mm.v

                        @blk.scalar
                        def _(e, m=m, mv=mv):
                            e.wait_ge(s_mm.h, mv)
                            s_ac.inc(e.activation(hb[:, m * 512:(m + 1) * 512], ps[m][:, 0:TB], AF.Gelu,
                                                  bias=b1c[:, m:m + 1], scale=1.0), 1)
                    acv = s_ac.v

                    st_done = {}
                    xst_done = {}
                    for m in range(8):
                        xbuf = xs_a if m % 2 == 0 else xs_d
                        scb = sc32 if m % 2 == 0 else sc32b

                        xsm = s_lda if m % 2 == 0 else s_ldb

                        @blk.sync
                        def _(e, m=m, xbuf=xbuf, xsm=xsm):
                            if m >= 2:
                                e.wait_ge(s_vv.h, st_done[m - 2])
                            xsm.inc(e.dma_start(xbuf[:], xT[m * 128:(m + 1) * 128, c0:c0 + TB]), 16)
                        ldv = (xsm, xsm.v)

                        @blk.tensor
                        def _(e, m=m, acv=acv):
                            e.wait_ge(s_ac.h, acv)
                            if m >= 2:
                                e.wait_ge(s_vv.h, st_done[m - 2])
                            for kk in range(2):
                                i = e.matmul(ps[2 + m % 2][:, 0:TB],
                                             w2tb[:, kk * 1024 + m * 128: kk * 1024 + (m + 1) * 128],
                                             hb[:, kk * 512:(kk + 1) * 512], start=(kk == 0), stop=(kk == 1))
                            s_mm.inc(i, 1)
                        mv = s_mm.v

                        @blk.scalar
                        def _(e, m=m, mv=mv):
                            e.wait_ge(s_mm.h, mv)
                            s_ac.inc(e.activation(gp1[:, 0:512] if m % 2 == 0 else gp1[:, 512:1024],
                                                  ps[2 + m % 2][:, 0:TB], AF.Identity,
                                                  bias=b2c[:, m:m + 1], scale=1.0), 1)
                        av = s_ac.v

                        @blk.vector
                        def _(e, m=m, av=av, ldv=ldv, xbuf=xbuf, scb=scb):
                            e.wait_ge(s_ac.h, av)
                            e.wait_ge(ldv[0].h, ldv[1])
                            if m >= 2:
                                e.wait_ge(s_st.h, xst_done[m - 2])
                            g = gp1[:, 0:512] if m % 2 == 0 else gp1[:, 512:1024]
                            e.tensor_tensor(scb[:], g, qf[:, m * 512:(m + 1) * 512], ALU.add)
                            s_vv.inc(e.tensor_tensor(scb[:], scb[:], xbuf[:], ALU.add), 1)
                            st_done[m] = s_vv.v
                        vv = s_vv.v

                        @blk.sync
                        def _(e, m=m, vv=vv, scb=scb):
                            e.wait_ge(s_vv.h, vv)
                            s_st.inc(e.dma_start(x_hbm[m * 128:(m + 1) * 128, c0:c0 + TB], scb[:]), 16)
                            xst_done[m] = s_st.v

                    @blk.sync
                    def _(e):
                        s_st.w(e)

    if upto >= 3:
        ln_pass(x_hbm)

        with nc.sbuf_tensor("hfull", [128, 24 * TC], BF16) as hfull, \
             nc.sbuf_tensor("fgw", [128, 1024], BF16) as fgw, \
             nc.sbuf_tensor("fuw", [128, 1024], BF16) as fuw, \
             nc.sbuf_tensor("fgw2", [128, 1024], BF16) as fgw2, \
             nc.sbuf_tensor("fuw2", [128, 1024], BF16) as fuw2, \
             nc.sbuf_tensor("fsl", [128, 512], BF16) as fsl, \
             nc.sbuf_tensor("fdw", [128, 3072], BF16) as fdw, \
             nc.sbuf_tensor("fdw2", [128, 3072], BF16) as fdw2:

            with NB() as blk:
                hdone = {}
                for m in range(24):
                    g_b, u_b = (fgw, fuw) if m % 2 == 0 else (fgw2, fuw2)
                    wsm = s_wta if m % 2 == 0 else s_wtb
                    g = (s_mm, hdone[m - 2][0]) if m >= 2 else None
                    stream_w(blk, g_b[:], rearr(wgT[:, m * 128:(m + 1) * 128]), guard=g, sem=wsm)
                    wsm, wv = stream_w(blk, u_b[:], rearr(wuT[:, m * 128:(m + 1) * 128]), guard=g, sem=wsm)

                    for tt4 in range(4):
                        c0 = tt4 * TB

                        @blk.tensor
                        def _(e, m=m, c0=c0, wv=wv, g_b=g_b, u_b=u_b, wsm=wsm):
                            e.wait_ge(wsm.h, wv)
                            e.wait_ge(s_vv.h, s_vv.v)
                            for kk in range(8):
                                e.matmul(ps[0][:, 0:TB], g_b[:, kk * 128:(kk + 1) * 128],
                                         xn[:, kk * TC + c0: kk * TC + c0 + TB], start=(kk == 0), stop=(kk == 7))
                            for kk in range(8):
                                i = e.matmul(ps[1][:, 0:TB], u_b[:, kk * 128:(kk + 1) * 128],
                                             xn[:, kk * TC + c0: kk * TC + c0 + TB], start=(kk == 0), stop=(kk == 7))
                            s_mm.inc(i, 1)
                        mv = s_mm.v

                        @blk.scalar
                        def _(e, mv=mv):
                            e.wait_ge(s_mm.h, mv)
                            s_ac.inc(e.activation(fsl[:], ps[0][:, 0:TB], AF.Silu, bias=0.0, scale=QS), 1)
                        av = s_ac.v

                        @blk.vector
                        def _(e, m=m, c0=c0, av=av):
                            e.wait_ge(s_ac.h, av)
                            s_vv.inc(e.tensor_tensor(hfull[:, m * TC + c0: m * TC + c0 + TB], fsl[:],
                                                     ps[1][:, 0:TB], ALU.mult), 1)
                    hdone[m] = (s_mm.v, s_vv.v)

            with NB() as blk:
                ddone = {}
                fstd = {}
                for m in range(8):
                    d_b = fdw if m % 2 == 0 else fdw2
                    wsm = s_wta if m % 2 == 0 else s_wtb
                    g = (s_mm, ddone[m - 2][0]) if m >= 2 else None
                    wsm, wv = stream_w(blk, d_b[:], rearr(wdT[:, m * 128:(m + 1) * 128]), guard=g, sem=wsm)
                    for tt4 in range(4):
                        c0 = tt4 * TB
                        xbuf = xs_a if tt4 % 2 == 0 else xs_d
                        xsm = s_lda if tt4 % 2 == 0 else s_ldb

                        @blk.sync
                        def _(e, m=m, c0=c0, xbuf=xbuf, xsm=xsm):
                            e.wait_ge(s_vv.h, max(0, s_vv.v - 1))
                            xsm.inc(e.dma_start(xbuf[:], x_hbm[m * 128:(m + 1) * 128, c0:c0 + TB]), 16)
                        ldv = (xsm, xsm.v)

                        @blk.tensor
                        def _(e, m=m, c0=c0, wv=wv, d_b=d_b, wsm=wsm):
                            e.wait_ge(wsm.h, wv)
                            e.wait_ge(s_vv.h, s_vv.v)
                            for kk in range(24):
                                i = e.matmul(ps[2][:, 0:TB], d_b[:, kk * 128:(kk + 1) * 128],
                                             hfull[:, kk * TC + c0: kk * TC + c0 + TB],
                                             start=(kk == 0), stop=(kk == 23))
                            s_mm.inc(i, 1)
                        mv = s_mm.v

                        scb = sc32 if tt4 % 2 == 0 else sc32b

                        @blk.vector
                        def _(e, mv=mv, ldv=ldv, xbuf=xbuf, scb=scb, m=m, tt4=tt4):
                            e.wait_ge(s_mm.h, mv)
                            e.wait_ge(ldv[0].h, ldv[1])
                            key = (m, tt4 - 2) if tt4 >= 2 else (m - 1, tt4 + 2)
                            if key in fstd:
                                e.wait_ge(s_st.h, fstd[key])
                            s_vv.inc(e.scalar_tensor_tensor(scb[:], ps[2][:, 0:TB], QS * QS, xbuf[:],
                                                            ALU.mult, ALU.add), 1)
                        vv = s_vv.v

                        @blk.sync
                        def _(e, m=m, c0=c0, vv=vv, scb=scb, tt4=tt4):
                            e.wait_ge(s_vv.h, vv)
                            s_st.inc(e.dma_start(x_hbm[m * 128:(m + 1) * 128, c0:c0 + TB], scb[:]), 16)
                            fstd[(m, tt4)] = s_st.v
                    ddone[m] = (s_mm.v, 0)

                @blk.sync
                def _(e):
                    s_st.w(e)

    if upto >= 4:
        ln_pass(x_hbm)

        with nc.sbuf_tensor("aw", [128, 1024], BF16) as aw, \
             nc.sbuf_tensor("aw2", [128, 1024], BF16) as aw2, \
             nc.sbuf_tensor("qkvt", [128, TC], BF16) as qkvt:
            with NB() as blk:
                qdone = {}
                for m in range(24):
                    a_b = aw if m % 2 == 0 else aw2
                    wsm = s_wta if m % 2 == 0 else s_wtb
                    g = (s_mm, qdone[m - 2][0]) if m >= 2 else None
                    wsm, wv = stream_w(blk, a_b[:], rearr(wattn[:, m * 128:(m + 1) * 128]), guard=g, sem=wsm)
                    for tt4 in range(4):
                        c0 = tt4 * TB

                        @blk.tensor
                        def _(e, m=m, c0=c0, wv=wv, a_b=a_b, tt4=tt4, wsm=wsm):
                            e.wait_ge(wsm.h, wv)
                            e.wait_ge(s_ac.h, max(0, s_ac.v - 1))
                            for kk in range(8):
                                i = e.matmul(ps[tt4 % 2][:, 0:TB], a_b[:, kk * 128:(kk + 1) * 128],
                                             xn[:, kk * TC + c0: kk * TC + c0 + TB], start=(kk == 0), stop=(kk == 7))
                            s_mm.inc(i, 1)
                        mv = s_mm.v

                        @blk.scalar
                        def _(e, m=m, c0=c0, mv=mv, tt4=tt4):
                            e.wait_ge(s_mm.h, mv)
                            if c0 == 0:
                                e.wait_ge(s_st.h, s_st.v)
                            s_ac.inc(e.activation(qkvt[:, c0:c0 + TB], ps[tt4 % 2][:, 0:TB], AF.Identity,
                                                  bias=battnc[:, m:m + 1], scale=1.0), 1)
                    av = s_ac.v
                    qdone[m] = (s_mm.v, av)

                    @blk.sync
                    def _(e, m=m, av=av):
                        e.wait_ge(s_ac.h, av)
                        s_st.inc(e.dma_start(ag1_in[m * 128:(m + 1) * 128, :], qkvt[:].bitcast(I32)), 16)
                stv = s_st.v

                @blk.gpsimd
                def _(e, stv=stv):
                    e.wait_ge(s_st.h, stv)
                    s_cc.inc(e.collective_compute("AllGather", ALU.bypass, RG, [ag1_in[:]], [ag1_out[:]]), 1)
                    s_cb.inc(e.collective_compute("AllReduce", ALU.add, RG, [br_in[:]], [br_out[:]]), 1)

    if upto >= 5:
        with ExitStack() as _st:
            _e = _st.enter_context
            qsl = _e(nc.sbuf_tensor("qsl", [128, TFULL], BF16))
            ksl = _e(nc.sbuf_tensor("ksl", [128, TFULL], BF16))
            vsl = _e(nc.sbuf_tensor("vsl", [128, TFULL], BF16))
            vtm = _e(nc.sbuf_tensor("vtm", [128, TFULL], BF16))
            p1 = _e(nc.sbuf_tensor("p1", [128, TFULL], BF16))
            p2 = _e(nc.sbuf_tensor("p2", [128, TFULL], BF16))
            pt1 = _e(nc.sbuf_tensor("pt1", [128, 512], BF16))
            pt2 = _e(nc.sbuf_tensor("pt2", [128, 512], BF16))
            asl = _e(nc.sbuf_tensor("asl", [128, 1024], F32))
            lsum = _e(nc.sbuf_tensor("lsum", [128, 20], F32))
            li1 = _e(nc.sbuf_tensor("li1", [128, 32], F32))
            li2 = _e(nc.sbuf_tensor("li2", [128, 32], F32))
            lt = _e(nc.sbuf_tensor("lt", [32, 128], F32))
            negm = _e(nc.sbuf_tensor("negm", [128, 1], F32))
            redc = _e(nc.sbuf_tensor("redc", [128, 2], F32))
            redr = _e(nc.sbuf_tensor("redr", [1, 128], F32))
            omine = _e(nc.sbuf_tensor("omine", [128, B * TFULL], BF16))
            psTb = _e(nc.psum_tensor("psTb4", [128, 512], BF16))
            psTf = _e(nc.psum_tensor("psTf4", [128, 128], F32))

            with NB() as blk:
                @blk.sync
                def _(e):
                    me = e.partition_id()
                    offa = e.snap(me * 256, min_val=0, max_val=7 * 256)
                    s_ld.inc(e.dma_start(asl[:, 0:512], a_hbm[bass.ds(offa, 128), :]), 16)
                    av2 = a_hbm[128:, :]
                    s_ld.inc(e.dma_start(asl[:, 512:1024], av2[bass.ds(offa, 128), :]), 16)
                    s_ld.inc(e.dma_start(sc32[:], mmask[:]), 16)
                ldv = s_ld.v

                @blk.sync
                def _(e):
                    e.wait_ge(s_cc.h, s_cc.v)
                    e.wait_ge(s_cb.h, s_cb.v)
                    me = e.partition_id()
                    off = e.snap(me * 128, min_val=0, max_val=7 * 128)
                    for r in range(NCORES):
                        src = ag1_out[r * 3072:(r + 1) * 3072, :].rearrange("(sel q) c -> sel q c", q=1024)
                        dst = myqkv[:, r * 1024:(r + 1) * 1024].rearrange("(sel p) c -> sel p c", p=128)
                        s_wt.inc(e.dma_start(dst, src[:, bass.ds(off, 128), :]), 16)
                    s_wt.w(e)

                @blk.vector
                def _(e, ldv=ldv):
                    e.wait_ge(s_ld.h, ldv)
                    e.tensor_tensor(asl[:, 0:512], asl[:, 0:512], sc32[:], ALU.add)
                    s_vv.inc(e.tensor_tensor(asl[:, 512:1024], asl[:, 512:1024], sc32[:], ALU.add), 1)

            for b in range(B):
                with NB() as blk:
                    @blk.sync
                    def _(e, b=b):
                        e.wait_ge(s_wt.h, s_wt.v)
                        for (dst, sel) in ((qsl, 0), (ksl, 1), (vsl, 2)):
                            for r in range(NCORES):
                                s_ld.inc(e.dma_start(
                                    dst[:, r * TB:(r + 1) * TB].bitcast(I32),
                                    myqkv[sel * 128:(sel + 1) * 128,
                                          r * 1024 + b * 256: r * 1024 + (b + 1) * 256]), 16)
                    ldv = s_ld.v

                    for sj in range(32):
                        @blk.tensor
                        def _(e, sj=sj, ldv=ldv):
                            if sj == 0:
                                e.wait_ge(s_ld.h, ldv)
                            else:
                                e.wait_ge(s_vv.h, s_vv.v)
                            s_mm.inc(e.transpose(psTb[:, 0:128], vsl[:, sj * 128:(sj + 1) * 128], ident[:]), 1)
                        mv = s_mm.v

                        @blk.vector
                        def _(e, sj=sj, mv=mv):
                            e.wait_ge(s_mm.h, mv)
                            s_vv.inc(e.tensor_copy(vtm[:, sj * 128:(sj + 1) * 128], psTb[:, 0:128]), 1)

                    @blk.vector
                    def _(e, ldv=ldv):
                        e.wait_ge(s_ld.h, ldv)
                        s_vv.inc(e.reduce_max(redc[:, 0:1], qsl[:], axis=AX.X, apply_absolute_value=True), 1)
                        s_vv.inc(e.reduce_max(redc[:, 1:2], ksl[:], axis=AX.X, apply_absolute_value=True), 1)
                    vv = s_vv.v

                    for col in range(2):
                        @blk.tensor
                        def _(e, col=col, vv=vv):
                            e.wait_ge(s_vv.h, s_vv.v)
                            s_mm.inc(e.transpose(psTf[0:1, 0:128], redc[:, col:col + 1], identf[:]), 1)
                        mv = s_mm.v

                        @blk.vector
                        def _(e, col=col, mv=mv):
                            e.wait_ge(s_mm.h, mv)
                            s_vv.inc(e.tensor_copy(redr[:], psTf[0:1, 0:128]), 1)
                            s_vv.w(e)
                            s_vv.inc(e.reduce_max(lsum[0:1, 18 + col:19 + col], redr[:], axis=AX.X), 1)
                            s_vv.w(e)

                    @blk.vector
                    def _(e):
                        e.tensor_tensor(lsum[0:1, 18:19], lsum[0:1, 18:19], lsum[0:1, 19:20], ALU.mult)
                        s_vv.inc(e.tensor_scalar(lsum[0:1, 18:19], lsum[0:1, 18:19], -8.0, None, ALU.mult), 1)
                    vv = s_vv.v

                    @blk.sync
                    def _(e, vv=vv):
                        e.wait_ge(s_vv.h, vv)
                        s_bc.inc(e.dma_start(sc_hbm[0:1, 0:1], lsum[0:1, 18:19]), 16)
                        s_bc.w(e)
                        s_bc.inc(e.dma_start(negm[:], sc_hbm[0:1, 0:1].broadcast_to((128, 1))), 16)
                    ldnm = s_bc.v

                    exp_done = {}
                    red_done = {}
                    oevac = 0
                    for rb in range(32):
                        t0 = rb * 128
                        if t0 < 384:
                            chunks = [(0, t0 + 128, 384 - t0)]
                        else:
                            nfar = (t0 - 384) // 512
                            rem = (t0 - 384) - nfar * 512
                            chunks = [(ci * 512, 512, None) for ci in range(nfar)]
                            if rem:
                                chunks.append((nfar * 512, rem, None))
                            chunks.append((t0 - 384, 512, 0))
                        nch_ = len(chunks)

                        for ci, (sst, wid, aoff) in enumerate(chunks):
                            pA = ps[2 * (ci % 2)]
                            pB = ps[2 * (ci % 2) + 1]
                            key = (rb, ci - 2)

                            @blk.tensor
                            def _(e, t0=t0, sst=sst, wid=wid, pA=pA, pB=pB, ci=ci, rb=rb, ldnm=ldnm):
                                if rb == 0 and ci == 0:
                                    e.wait_ge(s_bc.h, ldnm)
                                if ci >= 2:
                                    e.wait_ge(s_ac.h, exp_done[(rb, ci - 2)])
                                elif rb >= 1:
                                    e.wait_ge(s_ac.h, s_ac.v)
                                if ci == 0 and rb >= 1:
                                    e.wait_ge(s_vv.h, red_done[rb - 1])
                                e.matmul(pA[:, 0:wid], qsl[0:64, t0:t0 + 128], ksl[0:64, sst:sst + wid],
                                         start=True, stop=True, tile_position=(0, 0))
                                s_mm.inc(e.matmul(pB[:, 0:wid], qsl[64:128, t0:t0 + 128], ksl[64:128, sst:sst + wid],
                                                  start=True, stop=True, tile_position=(64, 0)), 1)
                            mv = s_mm.v

                            vv = None
                            if aoff is not None:
                                @blk.vector
                                def _(e, pA=pA, pB=pB, wid=wid, aoff=aoff, mv=mv):
                                    e.wait_ge(s_mm.h, mv)
                                    e.tensor_tensor(pA[:, 0:wid], pA[:, 0:wid], asl[:, aoff:aoff + wid], ALU.add)
                                    s_vv.inc(e.tensor_tensor(pB[:, 0:wid], pB[:, 0:wid],
                                                             asl[:, 512 + aoff:512 + aoff + wid], ALU.add), 1)
                                vv = s_vv.v

                            @blk.scalar
                            def _(e, pA=pA, pB=pB, sst=sst, wid=wid, ci=ci, mv=mv, vv=vv):
                                if vv is not None:
                                    e.wait_ge(s_vv.h, vv)
                                else:
                                    e.wait_ge(s_mm.h, mv)
                                e.activation(p1[:, sst:sst + wid], pA[:, 0:wid], AF.Exp, bias=negm[:],
                                             scale=0.125, accum_out=lsum[:, ci:ci + 1])
                                s_ac.inc(e.activation(p2[:, sst:sst + wid], pB[:, 0:wid], AF.Exp, bias=negm[:],
                                                      scale=0.125, accum_out=lsum[:, 9 + ci:10 + ci]), 1)
                            exp_done[(rb, ci)] = s_ac.v
                        acv = s_ac.v

                        @blk.vector
                        def _(e, rb=rb, nch_=nch_, acv=acv):
                            e.wait_ge(s_ac.h, acv)
                            s_vv.inc(e.reduce_sum(redc[:, 0:1], lsum[:, 0:nch_], axis=AX.X), 1)
                            s_vv.inc(e.reduce_sum(redc[:, 1:2], lsum[:, 9:9 + nch_], axis=AX.X), 1)
                            s_vv.w(e)
                            s_vv.inc(e.reciprocal(li1[:, rb:rb + 1], redc[:, 0:1]), 1)
                            s_vv.inc(e.reciprocal(li2[:, rb:rb + 1], redc[:, 1:2]), 1)
                            red_done[rb] = s_vv.v

                        nsj = rb + 1
                        for sj4 in range(0, nsj, 4):
                            hi = min(sj4 + 4, nsj)

                            @blk.tensor
                            def _(e, sj4=sj4, hi=hi, acv=acv, oevac=oevac):
                                e.wait_ge(s_ac.h, acv)
                                e.wait_ge(s_vv.h, s_vv.v)
                                if sj4 == 0:
                                    e.wait_ge(s_ac.h, max(acv, oevac))
                                for sj in range(sj4, hi):
                                    i = e.transpose(psTb[:, (sj - sj4) * 128:(sj - sj4 + 1) * 128],
                                                    p1[:, sj * 128:(sj + 1) * 128], ident[:])
                                s_mm.inc(i, 1)
                            mv = s_mm.v

                            @blk.vector
                            def _(e, sj4=sj4, hi=hi, mv=mv):
                                e.wait_ge(s_mm.h, mv)
                                s_vv.inc(e.tensor_copy(pt1[:, 0:(hi - sj4) * 128], psTb[:, 0:(hi - sj4) * 128]), 1)
                            vv = s_vv.v

                            @blk.tensor
                            def _(e, sj4=sj4, hi=hi, vv=vv):
                                e.wait_ge(s_vv.h, vv)
                                for sj in range(sj4, hi):
                                    i = e.transpose(psTb[:, (sj - sj4) * 128:(sj - sj4 + 1) * 128],
                                                    p2[:, sj * 128:(sj + 1) * 128], ident[:])
                                s_mm.inc(i, 1)
                            mv = s_mm.v

                            @blk.vector
                            def _(e, sj4=sj4, hi=hi, mv=mv):
                                e.wait_ge(s_mm.h, mv)
                                s_vv.inc(e.tensor_copy(pt2[:, 0:(hi - sj4) * 128], psTb[:, 0:(hi - sj4) * 128]), 1)
                            vv = s_vv.v

                            @blk.tensor
                            def _(e, sj4=sj4, hi=hi, nsj=nsj, vv=vv):
                                e.wait_ge(s_vv.h, vv)
                                for sj in range(sj4, hi):
                                    e.matmul(ps[4][0:64, 0:128], vtm[:, sj * 128: sj * 128 + 64],
                                             pt1[:, (sj - sj4) * 128:(sj - sj4 + 1) * 128],
                                             start=(sj == 0), stop=(sj == nsj - 1), tile_position=(0, 0))
                                    i = e.matmul(ps[5][64:128, 0:128], vtm[:, sj * 128 + 64:(sj + 1) * 128],
                                                 pt2[:, (sj - sj4) * 128:(sj - sj4 + 1) * 128],
                                                 start=(sj == 0), stop=(sj == nsj - 1), tile_position=(0, 64))
                                s_mm.inc(i, 1)
                        mv = s_mm.v

                        @blk.scalar
                        def _(e, b=b, t0=t0, mv=mv):
                            e.wait_ge(s_mm.h, mv)
                            e.copy(omine[0:64, b * TFULL + t0: b * TFULL + t0 + 128], ps[4][0:64, 0:128])
                            s_ac.inc(e.copy(omine[64:128, b * TFULL + t0: b * TFULL + t0 + 128],
                                            ps[5][64:128, 0:128]), 1)
                        oevac = s_ac.v

                    vvb = s_vv.v
                    for hh, lic in ((0, li1), (1, li2)):
                        @blk.tensor
                        def _(e, lic=lic, vvb=vvb, hh=hh):
                            e.wait_ge(s_vv.h, s_vv.v)
                            s_mm.inc(e.transpose(psTf[0:32, 0:128], lic[:], identf[:]), 1)
                        mv = s_mm.v

                        @blk.vector
                        def _(e, mv=mv, hh=hh):
                            e.wait_ge(s_mm.h, mv)
                            if hh == 1:
                                e.wait_ge(s_st.h, s_st.v)
                            s_vv.inc(e.tensor_copy(lt[:], psTf[0:32, 0:128]), 1)
                        vv = s_vv.v

                        @blk.sync
                        def _(e, b=b, hh=hh, vv=vv):
                            e.wait_ge(s_vv.h, vv)
                            s_st.inc(e.dma_start(
                                ag3_in[hh:hh + 1, b * TFULL:(b + 1) * TFULL].rearrange(
                                    "o (a c) -> (o a) c", c=128), lt[:]), 16)

            with NB() as blk:
                @blk.sync
                def _(e):
                    s_st.inc(e.dma_start(ag2_in[:, :], omine[:].bitcast(I32)), 16)
                ldv = s_st.v

                @blk.gpsimd
                def _(e, ldv=ldv):
                    e.wait_ge(s_st.h, s_st.v)
                    s_cc.inc(e.collective_compute("AllGather", ALU.bypass, RG, [ag2_in[:]], [ag2_out[:]]), 1)
                    s_cc.inc(e.collective_compute("AllGather", ALU.bypass, RG, [ag3_in[:]], [ag3_out[:]]), 1)
                    s_cb.inc(e.collective_compute("AllReduce", ALU.add, RG, [br_in[:]], [br_out[:]]), 1)

    if upto >= 6:
        with nc.sbuf_tensor("myl_sb", [16, TC], F32) as myl_sb, \
             nc.sbuf_tensor("ilb", [128, 512], F32) as ilb, \
             nc.sbuf_tensor("pw", [128, 1024], BF16) as pw, \
             nc.sbuf_tensor("pw2", [128, 1024], BF16) as pw2, \
             nc.sbuf_tensor("ynorm", [128, 8 * TC], BF16) as ynorm:

            with NB() as blk:
                @blk.scalar
                def _(e):
                    e.wait_ge(s_cc.h, s_cc.v)
                    e.wait_ge(s_cb.h, s_cb.v)
                    me = e.partition_id()
                    off256 = e.snap(me * 256, min_val=0, max_val=7 * 256)
                    off512 = e.snap(me * 512, min_val=0, max_val=7 * 512)
                    for r in range(NCORES):
                        src = ag2_out[r * 128:(r + 1) * 128, :].rearrange(
                            "p (bb c) -> p bb c", c=2048)
                        dst = ynorm[:, r * TC:(r + 1) * TC].bitcast(I32).rearrange(
                            "p (bb c) -> p bb c", c=256)
                        s_sg.inc(e.dma_start(dst, src[:, :, bass.ds(off256, 256)]), 16)
                    src3 = ag3_out[:, :].rearrange("h (bb c) -> h bb c", c=TFULL)
                    s_sg.inc(e.dma_start(
                        myl_sb[:, :].rearrange("h (bb c) -> h bb c", c=TB),
                        src3[:, :, bass.ds(off512, 512)]), 16)
                    s_sg.w(e)
                    s_sg.inc(e.dma_start(myl[:, :], myl_sb[:]), 16)
                ldall = s_sg.v

                ndone = {}
                for r in range(NCORES):
                    for bb in range(B):
                        @blk.sync
                        def _(e, r=r, bb=bb, ldall=ldall):
                            if r == 0 and bb == 0:
                                s_sg.w(e, ldall)
                            if not (r == 0 and bb == 0):
                                e.wait_ge(s_vv.h, ndone[(r, bb - 1) if bb else (r - 1, B - 1)])
                            for hh in range(2):
                                s_ld.inc(e.dma_start(
                                    ilb[hh * 64:(hh + 1) * 64, :],
                                    myl[2 * r + hh: 2 * r + hh + 1,
                                        bb * TB:(bb + 1) * TB].broadcast_to((64, TB))), 16)
                        ldv2 = s_ld.v

                        @blk.vector
                        def _(e, r=r, bb=bb, ldv2=ldv2, ldall=ldall):
                            e.wait_ge(s_sg.h, ldall)
                            e.wait_ge(s_ld.h, ldv2)
                            sl = ynorm[:, r * TC + bb * TB: r * TC + (bb + 1) * TB]
                            s_vv.inc(e.tensor_tensor(sl, sl, ilb[:], ALU.mult), 1)
                            ndone[(r, bb)] = s_vv.v

            with NB() as blk:
                pdone = {}
                pstd = {}
                for m in range(8):
                    p_b = pw if m % 2 == 0 else pw2
                    wsm = s_wta if m % 2 == 0 else s_wtb
                    g = (s_mm, pdone[m - 2][0]) if m >= 2 else None
                    wsm, wv = stream_w(blk, p_b[:], rearr(wproj[:, m * 128:(m + 1) * 128]), guard=g, sem=wsm)
                    for tt4 in range(4):
                        c0 = tt4 * TB
                        xbuf = xs_a if tt4 % 2 == 0 else xs_d
                        xsm = s_lda if tt4 % 2 == 0 else s_ldb

                        @blk.sync
                        def _(e, m=m, c0=c0, xbuf=xbuf, xsm=xsm):
                            e.wait_ge(s_vv.h, max(0, s_vv.v - 1))
                            xsm.inc(e.dma_start(xbuf[:], x_hbm[m * 128:(m + 1) * 128, c0:c0 + TB]), 16)
                        ldv = (xsm, xsm.v)

                        @blk.tensor
                        def _(e, m=m, c0=c0, wv=wv, p_b=p_b, wsm=wsm):
                            e.wait_ge(wsm.h, wv)
                            e.wait_ge(s_ac.h, s_ac.v)
                            for kk in range(8):
                                i = e.matmul(ps[0][:, 0:TB], p_b[:, kk * 128:(kk + 1) * 128],
                                             ynorm[:, kk * TC + c0: kk * TC + c0 + TB], start=(kk == 0), stop=(kk == 7))
                            s_mm.inc(i, 1)
                        mv = s_mm.v

                        @blk.scalar
                        def _(e, m=m, mv=mv):
                            e.wait_ge(s_mm.h, mv)
                            e.wait_ge(s_vv.h, s_vv.v)
                            s_ac.inc(e.activation(sc32b[:], ps[0][:, 0:TB], AF.Identity,
                                                  bias=bprojc[:, m:m + 1], scale=1.0), 1)
                        av = s_ac.v

                        scb = mb if tt4 % 2 == 0 else ib

                        @blk.vector
                        def _(e, av=av, ldv=ldv, xbuf=xbuf, scb=scb, m=m, tt4=tt4):
                            e.wait_ge(s_ac.h, av)
                            e.wait_ge(ldv[0].h, ldv[1])
                            key = (m, tt4 - 2) if tt4 >= 2 else (m - 1, tt4 + 2)
                            if key in pstd:
                                e.wait_ge(s_st.h, pstd[key])
                            s_vv.inc(e.tensor_tensor(scb[:], sc32b[:], xbuf[:], ALU.add), 1)
                        vv = s_vv.v

                        @blk.sync
                        def _(e, m=m, c0=c0, vv=vv, scb=scb, tt4=tt4):
                            e.wait_ge(s_vv.h, vv)
                            s_st.inc(e.dma_start(x_hbm[m * 128:(m + 1) * 128, c0:c0 + TB], scb[:]), 16)
                            pstd[(m, tt4)] = s_st.v
                    pdone[m] = (s_mm.v, 0)

                @blk.sync
                def _(e):
                    s_st.w(e)

    if upto >= 7:
        ln_pass(x_hbm)

        with nc.sbuf_tensor("rw_sb", [128, 128], BF16) as rw_sb, \
             nc.sbuf_tensor("lgf", [16, TC], F32) as lgf, \
             nc.sbuf_tensor("lgt", [128, 16], F32) as lgt, \
             nc.sbuf_tensor("top", [128, 8], F32) as top, \
             nc.sbuf_tensor("e1", [128, 16], F32) as e1, \
             nc.sbuf_tensor("e2", [128, 16], F32) as e2, \
             nc.sbuf_tensor("cmbf", [16, TC], F32) as cmbf, \
             nc.psum_tensor("psTf", [128, 128], F32) as psTf:
            with NB() as blk:
                wsm, wv = stream_w(blk, rw_sb[:], rearr(rwT[:]), sem=s_wta)
                for tt4 in range(4):
                    c0 = tt4 * TB

                    @blk.tensor
                    def _(e, c0=c0, wv=wv, tt4=tt4, wsm=wsm):
                        e.wait_ge(wsm.h, wv)
                        e.wait_ge(s_ac.h, s_ac.v)
                        for kk in range(8):
                            i = e.matmul(ps[0][0:16, 0:TB], rw_sb[:, kk * 16:(kk + 1) * 16],
                                         xn[:, kk * TC + c0: kk * TC + c0 + TB], start=(kk == 0), stop=(kk == 7))
                        s_mm.inc(i, 1)
                    mv = s_mm.v

                    @blk.scalar
                    def _(e, c0=c0, mv=mv):
                        e.wait_ge(s_mm.h, mv)
                        s_ac.inc(e.activation(lgf[:, c0:c0 + TB], ps[0][0:16, 0:TB], AF.Identity,
                                              bias=rbc[:], scale=1.0), 1)
                acv = s_ac.v

                for ti in range(16):
                    @blk.tensor
                    def _(e, ti=ti, acv=acv):
                        e.wait_ge(s_ac.h, acv)
                        e.wait_ge(s_vv.h, s_vv.v)
                        s_mm.inc(e.transpose(psTf[0:128, 0:16], lgf[:, ti * 128:(ti + 1) * 128], identf[0:16, 0:16]), 1)
                    mv = s_mm.v

                    @blk.vector
                    def _(e, ti=ti, mv=mv):
                        e.wait_ge(s_mm.h, mv)
                        s_vv.inc(e.tensor_copy(lgt[:], psTf[0:128, 0:16]), 1)
                        s_vv.w(e)
                        s_vv.inc(e.reduce_max(top[:, 0:1], lgt[:], axis=AX.X), 1)
                        s_vv.w(e)
                        e.tensor_tensor(e1[:], lgt[:], top[:, 0:1].broadcast_to((128, 16)), ALU.is_ge)
                        s_vv.inc(e.scalar_tensor_tensor(e2[:], e1[:], NEG, lgt[:], ALU.mult, ALU.add), 1)
                        s_vv.w(e)
                        s_vv.inc(e.reduce_max(top[:, 1:2], e2[:], axis=AX.X), 1)
                        s_vv.w(e)
                        e.tensor_tensor(e2[:], e2[:], top[:, 1:2].broadcast_to((128, 16)), ALU.is_ge)
                        s_vv.inc(e.tensor_tensor(top[:, 2:3], top[:, 1:2], top[:, 0:1], ALU.subtract), 1)
                    vv = s_vv.v

                    @blk.scalar
                    def _(e, vv=vv):
                        e.wait_ge(s_vv.h, vv)
                        e.activation(top[:, 3:4], top[:, 2:3], AF.Sigmoid)
                        s_ac.inc(e.activation(top[:, 4:5], top[:, 2:3], AF.Sigmoid, bias=0.0, scale=-1.0), 1)
                    av = s_ac.v

                    @blk.vector
                    def _(e, av=av):
                        e.wait_ge(s_ac.h, av)
                        e.tensor_tensor(e1[:], e1[:], top[:, 4:5].broadcast_to((128, 16)), ALU.mult)
                        e.tensor_tensor(e2[:], e2[:], top[:, 3:4].broadcast_to((128, 16)), ALU.mult)
                        s_vv.inc(e.tensor_tensor(lgt[:], e1[:], e2[:], ALU.add), 1)
                    vv = s_vv.v

                    @blk.tensor
                    def _(e, ti=ti, vv=vv):
                        e.wait_ge(s_vv.h, vv)
                        s_mm.inc(e.transpose(psTf[0:16, 0:128], lgt[:], identf[:]), 1)
                    mv = s_mm.v

                    @blk.vector
                    def _(e, ti=ti, mv=mv):
                        e.wait_ge(s_mm.h, mv)
                        s_vv.inc(e.tensor_copy(cmbf[:, ti * 128:(ti + 1) * 128], psTf[0:16, 0:128]), 1)
                vv = s_vv.v

                @blk.sync
                def _(e, vv=vv):
                    e.wait_ge(s_vv.h, vv)
                    s_st.inc(e.dma_start(cmb_hbm[:, :], cmbf[:]), 16)
                    s_st.w(e)

        with nc.sbuf_tensor("hall", [128, 72 * TB], BF16) as hall, \
             nc.sbuf_tensor("egw", [128, 4096], BF16) as egw, \
             nc.sbuf_tensor("euw", [128, 4096], BF16) as euw, \
             nc.sbuf_tensor("egw2", [128, 4096], BF16) as egw2, \
             nc.sbuf_tensor("euw2", [128, 4096], BF16) as euw2, \
             nc.sbuf_tensor("edw", [128, 4096], BF16) as edw, \
             nc.sbuf_tensor("edw2", [128, 4096], BF16) as edw2, \
             nc.sbuf_tensor("cbb", [128, 512], F32) as cbb, \
             nc.sbuf_tensor("fsl2", [128, 512], BF16) as fsl2, \
             nc.psum_tensor("pd6", [128, 512], F32) as pd6, \
             nc.psum_tensor("pd7", [128, 512], F32) as pd7:

            pall = ps + [pd6, pd7]
            mstd = {}
            for tt4 in range(4):
                c0 = tt4 * TB

                with NB() as blk:
                    mm_done = {}
                    for e_i in range(18):
                        g_b, u_b = (egw, euw) if e_i % 2 == 0 else (egw2, euw2)
                        wsm = s_wta if e_i % 2 == 0 else s_wtb
                        gd = (s_mm, mm_done[e_i - 2]) if e_i >= 2 else None
                        if e_i < 16:
                            stream_w(blk, g_b[:], rearr(gate_p[e_i]), guard=gd, sem=wsm)
                            wsm, wv = stream_w(blk, u_b[:], rearr(up_p[e_i]), guard=gd, sem=wsm)
                        else:
                            half = e_i - 16
                            stream_w(blk, g_b[:], rearr(shgT[:, half * 512:(half + 1) * 512]), guard=gd, sem=wsm)
                            wsm, wv = stream_w(blk, u_b[:], rearr(shuT[:, half * 512:(half + 1) * 512]), guard=gd, sem=wsm)

                        ldv = 0
                        if e_i < 16:
                            @blk.sync
                            def _(e, e_i=e_i, c0=c0):
                                e.wait_ge(s_vv.h, s_vv.v)
                                s_ld.inc(e.dma_start(cbb[:], cmb_hbm[e_i:e_i + 1, c0:c0 + TB].broadcast_to((128, TB))), 16)
                            ldv = s_ld.v

                        for m in range(4):
                            hc = e_i * 4 + m if e_i < 16 else 64 + (e_i - 16) * 4 + m

                            @blk.tensor
                            def _(e, m=m, c0=c0, wv=wv, g_b=g_b, u_b=u_b, wsm=wsm):
                                e.wait_ge(wsm.h, wv)
                                e.wait_ge(s_vv.h, s_vv.v)
                                for kk in range(8):
                                    e.matmul(ps[0][:, 0:TB], g_b[:, kk * 512 + m * 128: kk * 512 + (m + 1) * 128],
                                             xn[:, kk * TC + c0: kk * TC + c0 + TB], start=(kk == 0), stop=(kk == 7))
                                for kk in range(8):
                                    i = e.matmul(ps[1][:, 0:TB], u_b[:, kk * 512 + m * 128: kk * 512 + (m + 1) * 128],
                                                 xn[:, kk * TC + c0: kk * TC + c0 + TB], start=(kk == 0), stop=(kk == 7))
                                s_mm.inc(i, 1)
                            mv = s_mm.v
                            mm_done[e_i] = mv

                            @blk.scalar
                            def _(e, mv=mv):
                                e.wait_ge(s_mm.h, mv)
                                s_ac.inc(e.activation(fsl2[:], ps[0][:, 0:TB], AF.Silu), 1)
                            av = s_ac.v

                            @blk.vector
                            def _(e, hc=hc, av=av, ldv=ldv, e_i=e_i):
                                e.wait_ge(s_ac.h, av)
                                if e_i < 16:
                                    e.wait_ge(s_ld.h, ldv)
                                e.tensor_tensor(sc32[:], fsl2[:], ps[1][:, 0:TB], ALU.mult)
                                if e_i < 16:
                                    s_vv.inc(e.tensor_tensor(hall[:, hc * TB:(hc + 1) * TB], sc32[:], cbb[:], ALU.mult), 1)
                                else:
                                    s_vv.inc(e.tensor_copy(hall[:, hc * TB:(hc + 1) * TB], sc32[:]), 1)

                with NB() as blk:
                    wt_done = {}
                    for e_i in range(18):
                        d_b = edw if e_i % 2 == 0 else edw2
                        wsm = s_wta if e_i % 2 == 0 else s_wtb
                        gd = (s_mm, wt_done[e_i - 2]) if e_i >= 2 else None
                        if e_i < 16:
                            wsm, wv = stream_w(blk, d_b[:], rearr(down_p[e_i]), guard=gd, sem=wsm)
                        else:
                            half = e_i - 16
                            wsm, wv = stream_w(blk, d_b[:], rearr(shdT[half * 512:(half + 1) * 512, :]), guard=gd, sem=wsm)

                        @blk.tensor
                        def _(e, e_i=e_i, wv=wv, d_b=d_b, wsm=wsm):
                            e.wait_ge(wsm.h, wv)
                            for kk in range(4):
                                hc = e_i * 4 + kk if e_i < 16 else 64 + (e_i - 16) * 4 + kk
                                for m in range(8):
                                    i = e.matmul(pall[m][:, 0:TB],
                                                 d_b[:, kk * 1024 + m * 128: kk * 1024 + (m + 1) * 128],
                                                 hall[:, hc * TB:(hc + 1) * TB],
                                                 start=(e_i == 0 and kk == 0),
                                                 stop=(e_i == 17 and kk == 3))
                            s_mm.inc(i, 1)
                        wt_done[e_i] = s_mm.v
                    mv = s_mm.v

                    for m in range(8):
                        xbuf = xs_a if m % 2 == 0 else xs_d
                        xsm = s_lda if m % 2 == 0 else s_ldb

                        @blk.sync
                        def _(e, m=m, c0=c0, xbuf=xbuf, xsm=xsm):
                            e.wait_ge(s_vv.h, max(0, s_vv.v - 1))
                            xsm.inc(e.dma_start(xbuf[:], x_hbm[m * 128:(m + 1) * 128, c0:c0 + TB]), 16)
                        ldv = (xsm, xsm.v)

                        scb = sc32 if m % 2 == 0 else sc32b

                        @blk.vector
                        def _(e, m=m, mv=mv, ldv=ldv, xbuf=xbuf, scb=scb):
                            e.wait_ge(s_mm.h, mv)
                            e.wait_ge(ldv[0].h, ldv[1])
                            if (m - 2, tt4) in mstd:
                                e.wait_ge(s_st.h, mstd[(m - 2, tt4)])
                            elif m < 2 and tt4 >= 1:
                                e.wait_ge(s_st.h, mstd[(6 + m, tt4 - 1)])
                            s_vv.inc(e.tensor_tensor(scb[:], pall[m][:, 0:TB], xbuf[:], ALU.add), 1)
                        vv = s_vv.v

                        @blk.sync
                        def _(e, m=m, c0=c0, vv=vv, scb=scb):
                            e.wait_ge(s_vv.h, vv)
                            s_st.inc(e.dma_start(yout[m * 128:(m + 1) * 128, c0:c0 + TB], scb[:]), 16)
                            mstd[(m, tt4)] = s_st.v

                    @blk.sync
                    def _(e):
                        s_st.w(e)

        with NB() as blk:
            @blk.sync
            def _(e):
                s_st.w(e)
    else:
        with NB() as blk:
            @blk.sync
            def _(e):
                src = x_hbm if upto >= 2 else xT
                for kk in range(NCH):
                    for tt4 in range(4):
                        c0 = tt4 * TB
                        s_ld.inc(e.dma_start(xs_a[:], src[kk * 128:(kk + 1) * 128, c0:c0 + TB]), 16)
                        s_ld.w(e)
                        s_st.inc(e.dma_start(yout[kk * 128:(kk + 1) * 128, c0:c0 + TB], xs_a[:]), 16)
                        s_st.w(e)

    return nc


def _bucket(d):
    if d < 16:
        return d
    v = 16 + int(np.float32(np.log(np.float32(d) / np.float32(16.0))
                            / np.float32(math.log(8.0)) * np.float32(16.0)))
    return min(v, 31)


def _host_consts():
    ident = np.eye(128, dtype=np.float32)
    pj = np.arange(128)[:, None] + 384 - np.arange(512)[None, :]
    mmask = np.where(pj < 0, np.float32(NEG), np.float32(0.0)).astype(np.float32)
    onehot = np.zeros((32, 128 * 512), dtype=np.float32)
    col = 0
    for p in range(128):
        for j in range(512):
            d = p + 384 - j
            if d >= 0:
                q = _bucket(d)
                onehot[q, col] += 8.0
                onehot[31, col] -= 8.0
            col += 1
    return ident, mmask, onehot


def kernel(**inp):
    f32 = np.float32
    c = np.ascontiguousarray
    ident, mmask, onehot = _host_consts()

    weights = {
        "tkT": c(np.asarray(inp["L0_attn_t_k"], f32).T),
        "tvT": c(np.asarray(inp["L0_attn_t_v"], f32).T),
        "tqT": c(np.asarray(inp["L0_attn_t_q"], f32).T),
        "w1T": c(np.asarray(inp["L0_attn_W1"], f32).T),
        "w2T": c(np.asarray(inp["L0_attn_W2"], f32).T),
        "b1h": c(np.asarray(inp["L0_attn_b1"], f32).reshape(2, 128).T),
        "b2h": c(np.asarray(inp["L0_attn_b2"], f32).reshape(8, 128).T),
        "wgT": c(np.asarray(inp["L0_ffn_w_gate"]).T.astype(np.int32)),
        "wuT": c(np.asarray(inp["L0_ffn_w_up"]).T.astype(np.int32)),
        "wdT": c(np.asarray(inp["L0_ffn_w_down"]).T.astype(np.int32)),
        "wattn": c(np.asarray(inp["L1_attn_W_attn"], f32)),
        "battn": c(np.asarray(inp["L1_attn_b_attn"], f32).reshape(24, 128).T),
        "wproj": c(np.asarray(inp["L1_attn_W_proj"], f32)),
        "bproj": c(np.asarray(inp["L1_attn_b_proj"], f32).reshape(8, 128).T),
        "btab": c(np.asarray(inp["L1_attn_bias_table"], f32)),
        "rwT": c(np.asarray(inp["L1_ffn_router_w"], f32).T),
        "rbias": c(np.asarray(inp["L1_ffn_router_bias"], f32).reshape(16, 1)),
        "up_p": c(np.asarray(inp["L1_ffn_up_proj"], f32)),
        "gate_p": c(np.asarray(inp["L1_ffn_gate_proj"], f32)),
        "down_p": c(np.asarray(inp["L1_ffn_down_proj"], f32)),
        "shgT": c(np.asarray(inp["L1_ffn_sh_gate"], f32).T),
        "shuT": c(np.asarray(inp["L1_ffn_sh_up"], f32).T),
        "shdT": c(np.asarray(inp["L1_ffn_sh_down"], f32).T),
        "ident_in": ident,
        "mmask": mmask,
        "onehotT": onehot,
    }

    x = np.asarray(inp["L0_attn_in_seq"], f32)
    in_maps = []
    for core in range(NCORES):
        m = dict(weights)
        m["xT"] = c(x[:, core * TB:(core + 1) * TB, :].reshape(TC, D).T)
        in_maps.append(m)

    nc = build_nc(upto=PH_ALL)
    res = run_bass_kernel_spmd(nc, in_maps, core_ids=list(range(NCORES)))

    out = np.empty((B, TFULL, D), dtype=f32)
    for core in range(NCORES):
        yc = res.results[core]["yout"]
        out[:, core * TB:(core + 1) * TB, :] = yc.T.reshape(B, TB, D)
    return out
